# version 03
# baseline (speedup 1.0000x reference)
from contextlib import ExitStack

import numpy as np
import ml_dtypes
import concourse.bass as bass
import concourse.mybir as mybir
from concourse.bass_utils import run_bass_kernel_spmd

F32 = mybir.dt.float32
BF16 = mybir.dt.bfloat16
I32 = mybir.dt.int32
AF = mybir.ActivationFunctionType
ALU = mybir.AluOpType
AX = mybir.AxisListType
BF = ml_dtypes.bfloat16

T = 2048
D = 1024
NT = 16
NC = 8
H0 = 4096
I1 = 2816
NI1 = 22

N_SUB = 6
DEBUG_TAPS = False
ATTN_STOP = 99
ATTN_PARTS = {"qk", "norm", "v"}

ENGINES = ("tensor", "scalar", "vector", "gpsimd", "sync")


def attn_tile_sets(layer):
    p = np.arange(128)[:, None]
    f = np.arange(512)[None, :]
    out = []
    for qt in range(4):
        row = []
        if layer == 0:
            kts = range(0, 4 * qt + 4)
        else:
            w = [kt for kt in range(max(0, 4 * qt - 2), 4 * qt + 4)]
            kts = sorted(set(([0] if 0 not in w else []) + w))
        for kt in kts:
            k = kt * 128 + p
            q = qt * 512 + f
            if layer == 0:
                valid = (k <= q)
            else:
                valid = (k <= q) & ((k >= q - 255) | (k <= 3))
            if valid.all():
                row.append((kt, None))
            else:
                row.append((kt, valid.astype(np.uint8).tobytes()))
        out.append(row)
    return out


def attn_mask_bank(layer):
    sets_ = attn_tile_sets(layer)
    patterns, index = [], {}
    for qt, row in enumerate(sets_):
        for kt, mb in row:
            if mb is None:
                continue
            if mb not in index:
                index[mb] = len(patterns)
                patterns.append(np.frombuffer(mb, dtype=np.uint8).reshape(128, 512))
    bank = np.stack(patterns) if patterns else np.zeros((1, 128, 512), np.uint8)
    tile_mask = {}
    for qt, row in enumerate(sets_):
        for kt, mb in row:
            tile_mask[(qt, kt)] = None if mb is None else index[mb]
    return bank.astype(BF), tile_mask


class Sem:
    def __init__(self, nc, ctx, name):
        self.h = ctx.enter_context(nc.semaphore(name))
        self.n = 0


class Prog:

    def __init__(self, nc, ctx):
        self.nc = nc
        self.ops = {e: [] for e in ENGINES}
        self.sems = {n: Sem(nc, ctx, n) for n in
                     ("sd", "sp", "ss", "sv", "sg", "so", "sw0", "sw1", "sab", "sxf", "sg0", "sg1", "scc", "sh0", "sh1")}
        self.audit = {e: [] for e in ENGINES}
        self.nblocks = 0

    def emit(self, e, fn):
        self.ops[e].append(fn)

    def wait(self, e, sem, val=None):
        v = sem.n if val is None else val
        if v > 0:
            self.audit[e].append(("wait", sem, v))
            self.emit(e, lambda eng, s=sem.h, vv=v: eng.wait_ge(s, vv))

    def dma(self, e, sem, out, in_):
        sem.n += 16
        self.audit[e].append(("inc", sem, 16))
        self.emit(e, lambda eng, o=out, i=in_, s=sem.h:
                  eng.dma_start(out=o, in_=i).then_inc(s, 16))
        return sem.n

    def inc(self, e, sem, fn):
        sem.n += 1
        self.audit[e].append(("inc", sem, 1))
        self.emit(e, lambda eng, f=fn, s=sem.h: f(eng).then_inc(s, 1))
        return sem.n

    def run(self, e, fn):
        self.emit(e, fn)

    def step(self, e, fn):
        m = {"vector": "sv", "scalar": "ss", "gpsimd": "sg", "tensor": "sp"}[e]
        sm = self.sems[m]
        self.wait(e, sm, sm.n)
        self.inc(e, sm, fn)
        self.wait(e, sm, sm.n)

    def check_deadlock(self):
        pos = {e: 0 for e in ENGINES}
        cnt = {id(s): 0 for s in self.sems.values()}
        logs = self.audit
        while True:
            prog_found = False
            for e in ENGINES:
                while pos[e] < len(logs[e]):
                    kind, sem, val = logs[e][pos[e]]
                    if kind == "inc":
                        cnt[id(sem)] += val
                        pos[e] += 1
                        prog_found = True
                    elif kind == "wait":
                        if cnt[id(sem)] >= val:
                            pos[e] += 1
                            prog_found = True
                        else:
                            break
                    else:
                        bid = val
                        ok = all(
                            not any(k == "barrier" and v < bid
                                    for k, _, v in logs[e2][pos[e2]:])
                            for e2 in ENGINES)
                        if ok:
                            pos[e] += 1
                            prog_found = True
                        else:
                            break
            if all(pos[e] >= len(logs[e]) for e in ENGINES):
                return True
            if not prog_found:
                print("DEADLOCK detected; engine heads:")
                names = {id(s): n for n, s in self.sems.items()}
                for e in ENGINES:
                    if pos[e] < len(logs[e]):
                        kind, sem, val = logs[e][pos[e]]
                        cur = cnt[id(sem)] if sem is not None else None
                        print(f"  {e}[{pos[e]}/{len(logs[e])}]: {kind} "
                              f"{names.get(id(sem), '?') if sem is not None else ''} "
                              f"need {val} have {cur}")
                return False

    def mms(self, sem, fns):
        for f in fns[:-1]:
            self.run("tensor", f)
        return self.inc("tensor", sem, fns[-1])

    def flush(self):
        for e in ENGINES:
            self.audit[e].append(("barrier", None, self.nblocks))
        self.nblocks += 1
        ops = self.ops
        self.ops = {e: [] for e in ENGINES}
        with self.nc.Block() as blk:
            for name in ENGINES:
                lst = ops[name]
                if lst:
                    getattr(blk, name)(lambda eng, lst=lst: [f(eng) for f in lst])


def build_program(n_sub=N_SUB):
    nc = bass.Bass("TRN2", target_bir_lowering=False)
    ctx = ExitStack()
    dram, sbuf = {}, {}
    uid = [0]

    def din(name, shape, dt=F32):
        dram[name] = nc.dram_tensor(name, shape, dt, kind="ExternalInput")

    def dout(name, shape, dt=F32):
        dram[name] = nc.dram_tensor(name, shape, dt, kind="ExternalOutput")

    def dtmp(name, shape, dt):
        dram[name] = nc.dram_tensor(name, shape, dt)

    def sb(name, shape, dt):
        if name not in sbuf:
            sbuf[name] = ctx.enter_context(nc.sbuf_tensor("sb_" + name, shape, dt))
        return sbuf[name]

    def mk_local(lctx):
        uid[0] += 1
        tag = f"p{uid[0]}"

        def lsb(name, shape, dt):
            return lctx.enter_context(nc.sbuf_tensor(f"{tag}_{name}", shape, dt))

        def lps(name, shape, dt=F32):
            return lctx.enter_context(nc.psum_tensor(f"{tag}_{name}", shape, dt))
        return lsb, lps

    din("x_in", [T, D])
    din("w1T", [D, H0]); din("w1n", [H0, D])
    din("w2T", [H0, D]); din("w2n", [D, H0])
    din("b1pp", [128, 32]); din("b2row", [1, D])
    for w in ("wqT", "wkT", "wvT", "woT", "aqT", "akT", "avT", "apT"):
        din(w, [D, D])
    for w in ("wqTs", "wkTs", "wvTs", "aqTs", "akTs", "avTs"):
        din(w, [8, D, 128])
    for b in ("bqrow", "bkrow", "bvrow", "borow"):
        din(b, [1, D])
    din("wgT", [D, I1]); din("wuT", [D, I1]); din("wdT", [I1, D])
    din("wgn", [I1, D]); din("wun", [I1, D]); din("wdn", [D, I1])
    din("clips", [1, 2])
    din("identity", [128, 128], BF16)
    din("identityf", [128, 128])
    din("ropeCq", [128, T], BF16); din("ropeSq", [128, T], BF16)
    nm0 = attn_mask_bank(0)[0].shape[0]
    nm1 = attn_mask_bank(1)[0].shape[0]
    din("mask0", [nm0, 128, 512], BF16)
    din("mask1", [nm1, 128, 512], BF16)
    din("constf", [128, 8])
    din("ones1", [1, 128], BF16)
    din("ones1f", [1, 128])
    din("onesrow", [1, 512], BF16)
    din("ones2", [128, 2], BF16)
    dout("y", [T, D])
    if DEBUG_TAPS:
        dout("dbg2_xnTf", [D, T])
        dout("dbg2_h", [I1, T], BF16)
        dout("dbg2_g", [D, I1], BF16)
        dout("dbg2_u", [D, I1], BF16)
        dout("dbg2_d", [I1, D], BF16)
        dout("dbg2_xq", [128, NC * T], BF16)
        dout("dbg2_scal", [1, 16])


    dtmp("scr_w1s", [D, H0], BF16)
    dtmp("scr_w2s", [H0, D], BF16)
    dtmp("scr_g", [D, I1], BF16)
    dtmp("scr_u", [D, I1], BF16)
    dtmp("scr_d", [I1, D], BF16)
    dtmp("scr_xnT", [D, T], F32)
    dtmp("scr_h", [I1, T], BF16)
    dtmp("cc_in", [1, 16], F32)
    dtmp("cc_out", [1, 16], F32)

    sb("x", [128, NT * D], F32)
    sb("xnT", [128, NC * T], BF16)
    sb("ident", [128, 128], BF16)
    sb("identf", [128, 128], F32)
    sb("constf", [128, 8], F32)
    sb("ones1", [1, 128], BF16)
    sb("onesrow", [1, 512], BF16)
    sb("ones2", [128, 2], BF16)
    sb("st_sum", [128, NT], F32)
    sb("st_sq", [128, NT], F32)
    sb("st_a", [128, NT], F32)
    sb("st_b", [128, NT], F32)
    sb("st_t", [128, NT], F32)
    sb("st_m", [128, NT], F32)
    sb("dump", [128, D], BF16)
    sb("alpha1", [128, 32], F32)
    sb("b1pp", [128, 32], F32)
    sb("b2rowb", [1, D], BF16)
    sb("bqrowb", [1, D], BF16)
    sb("bkrowb", [1, D], BF16)
    sb("bvrowb", [1, D], BF16)
    sb("borowb", [1, D], BF16)
    sb("onescol", [128, 1], BF16)
    sb("xn0", [128, D], BF16)
    sb("xn1", [128, D], BF16)
    sb("sgu_sc", [128, 44], F32)
    sb("sgx", [128, 44], F32)
    sb("sdrow", [128, 8], F32)
    sb("scal", [1, 16], F32)

    prog = Prog(nc, ctx)
    S = prog.sems
    cst = sbuf["constf"]
    C_HALF, C_ONE, C_NEG1 = cst[:, 0:1], cst[:, 1:2], cst[:, 2:3]

    for j in range(NT):
        prog.dma("sync", S["sd"], sbuf["x"][:, j * D:(j + 1) * D],
                 dram["x_in"][j * 128:(j + 1) * 128, :])
    for nm, dst in (("identity", "ident"), ("identityf", "identf"),
                    ("constf", "constf"), ("ones1", "ones1"),
                    ("onesrow", "onesrow"), ("ones2", "ones2"), ("b1pp", "b1pp")):
        prog.dma("sync", S["sd"], sbuf[dst][:, :], dram[nm][:, :])
    prog.dma("gpsimd", S["sg"], sbuf["b2rowb"][:, :], dram["b2row"][:, :])
    for src, dst in (("bqrow", "bqrowb"), ("bkrow", "bkrowb"),
                     ("bvrow", "bvrowb"), ("borow", "borowb")):
        prog.dma("gpsimd", S["sg"], sbuf[dst][:, :], dram[src][:, :])
    prog.wait("sync", S["sd"])
    prog.wait("gpsimd", S["sg"])
    prog.run("vector", lambda e: e.memset(sbuf["onescol"][:, :], 1.0))
    prog.flush()

    def emit_ln(fp32_scratch=False):
        for j in range(NT):
            tile = sbuf["x"][:, j * D:(j + 1) * D]
            prog.wait("scalar", S["ss"])
            prog.inc("scalar", S["ss"], lambda e, t=tile, j=j: e.activation(
                out=sbuf["dump"][:, :], in_=t, func=AF.Square,
                accum_out=sbuf["st_sq"][:, j:j + 1]))
            prog.inc("vector", S["sv"], lambda e, t=tile, j=j: e.reduce_sum(
                sbuf["st_sum"][:, j:j + 1], t, axis=AX.X))
        prog.wait("vector", S["ss"])
        prog.step("vector", lambda e: e.tensor_scalar(
            out=sbuf["st_m"][:, :], in0=sbuf["st_sum"][:, :], scalar1=1.0 / D,
            scalar2=None, op0=ALU.mult))
        prog.step("vector", lambda e: e.tensor_scalar(
            out=sbuf["st_t"][:, :], in0=sbuf["st_sq"][:, :], scalar1=1.0 / D,
            scalar2=None, op0=ALU.mult))
        prog.step("vector", lambda e: e.tensor_tensor(
            out=sbuf["dump"][:, 0:NT], in0=sbuf["st_m"][:, :], in1=sbuf["st_m"][:, :],
            op=ALU.mult))
        prog.step("vector", lambda e: e.tensor_sub(
            sbuf["st_t"][:, :], sbuf["st_t"][:, :], sbuf["dump"][:, 0:NT]))
        prog.step("vector", lambda e: e.tensor_scalar(
            out=sbuf["st_t"][:, :], in0=sbuf["st_t"][:, :], scalar1=1e-5,
            scalar2=None, op0=ALU.add))
        prog.wait("scalar", S["sv"])
        prog.step("scalar", lambda e: e.activation(
            out=sbuf["dump"][:, 0:NT], in_=sbuf["st_t"][:, :], func=AF.Sqrt))
        prog.wait("vector", S["ss"])
        prog.step("vector", lambda e: e.reciprocal(
            sbuf["st_a"][:, :], sbuf["dump"][:, 0:NT]))
        prog.step("vector", lambda e: e.tensor_mul(
            sbuf["st_b"][:, :], sbuf["st_m"][:, :], sbuf["st_a"][:, :]))
        prog.step("vector", lambda e: e.tensor_scalar(
            out=sbuf["st_b"][:, :], in0=sbuf["st_b"][:, :], scalar1=-1.0,
            scalar2=None, op0=ALU.mult))
        if fp32_scratch:
            prog.run("vector", lambda e: e.memset(sbuf["st_t"][:, 0:1], 0.0))
        prog.flush()

        with ExitStack() as lctx:
            lsb, lps = mk_local(lctx)
            xn = [sbuf["xn0"], sbuf["xn1"]]
            tp = [lps("tp0", [128, D], BF16), lps("tp1", [128, D], BF16)]
            if fp32_scratch:
                xnf = lsb("xnf", [128, D], F32)
                xnfT = lsb("xnfT", [128, D], F32)
                tpf = [lps("tpf0", [128, D], F32), lps("tpf1", [128, D], F32)]
            xnT = sbuf["xnT"]
            xn_free = [0, 0]
            tp_free = [0, 0]
            tpf_free = [0, 0]
            xnf_free = 0
            xnfT_free = 0
            for j in range(NT):
                b = j % 2
                tile = sbuf["x"][:, j * D:(j + 1) * D]
                if xn_free[b]:
                    prog.wait("vector", S["sp"], xn_free[b])
                if fp32_scratch:
                    if xnf_free:
                        prog.wait("vector", S["sp"], xnf_free)
                    prog.inc("vector", S["sv"], lambda e, t=tile, j=j: e.tensor_scalar(
                        out=xnf[:, :], in0=t, scalar1=sbuf["st_a"][:, j:j + 1],
                        scalar2=sbuf["st_b"][:, j:j + 1], op0=ALU.mult, op1=ALU.add))
                    prog.wait("vector", S["sv"])
                    prog.inc("vector", S["sv"], lambda e, b=b: e.tensor_copy(
                        out=xn[b][:, :], in_=xnf[:, :]))
                    prog.inc("vector", S["sv"], lambda e: e.reduce_max(
                        sbuf["st_m"][:, 0:1], xnf[:, :], axis=AX.X,
                        apply_absolute_value=True))
                    prog.wait("vector", S["sv"])
                    prog.inc("vector", S["sv"], lambda e: e.tensor_max(
                        sbuf["st_t"][:, 0:1], sbuf["st_t"][:, 0:1], sbuf["st_m"][:, 0:1]))
                    prog.wait("vector", S["sv"])
                else:
                    prog.inc("vector", S["sv"], lambda e, t=tile, j=j, b=b: e.tensor_scalar(
                        out=xn[b][:, :], in0=t, scalar1=sbuf["st_a"][:, j:j + 1],
                        scalar2=sbuf["st_b"][:, j:j + 1], op0=ALU.mult, op1=ALU.add))
                v_ready = S["sv"].n
                prog.wait("tensor", S["sv"], v_ready)
                if tp_free[b]:
                    prog.wait("tensor", S["sv"], tp_free[b])
                prog.mms(S["sp"], [
                    (lambda e, b=b, c=c: e.transpose(
                        tp[b][:, c * 128:(c + 1) * 128], xn[b][:, c * 128:(c + 1) * 128],
                        sbuf["ident"][:, :])) for c in range(NC)])
                xn_free[b] = S["sp"].n
                prog.wait("vector", S["sp"], S["sp"].n)
                prog.inc("vector", S["sv"], lambda e, b=b, j=j: e.tensor_copy(
                    out=bass.AP(xnT, j * 128, [[NC * T, 128], [T, NC], [1, 128]]),
                    in_=bass.AP(tp[b], 0, [[D, 128], [128, NC], [1, 128]])))
                tp_free[b] = S["sv"].n
                if fp32_scratch:
                    fb = j % 2
                    if tpf_free[fb]:
                        prog.wait("tensor", S["sv"], tpf_free[fb])
                    prog.mms(S["sp"], [
                        (lambda e, fb=fb, c=c: e.transpose(
                            tpf[fb][:, c * 128:(c + 1) * 128], xnf[:, c * 128:(c + 1) * 128],
                            sbuf["identf"][:, :])) for c in range(NC)])
                    xnf_free = S["sp"].n
                    if xnfT_free:
                        prog.wait("vector", S["sxf"], xnfT_free)
                    prog.wait("vector", S["sp"], S["sp"].n)
                    prog.inc("vector", S["sv"], lambda e, fb=fb: e.tensor_copy(
                        out=xnfT[:, :], in_=tpf[fb][:, :]))
                    tpf_free[fb] = S["sv"].n
                    prog.wait("sync", S["sv"], S["sv"].n)
                    xnfT_free = prog.dma(
                        "sync", S["sxf"],
                        bass.AP(dram["scr_xnT"], j * 128, [[T, 128], [T * 128, NC], [1, 128]]),
                        bass.AP(xnfT, 0, [[D, 128], [128, NC], [1, 128]]))
            prog.flush()

    def emit_ffn0_prep():
        with ExitStack() as lctx:
            lsb, lps = mk_local(lctx)
            wtmp = lsb("wtmp", [128, H0], F32)
            wsg = lsb("wsg", [128, D], F32)
            wsgb = lsb("wsgb", [128, H0], BF16)
            a2pp = lsb("a2pp", [128, 8], F32)
            a2row = lsb("a2row", [1, D], F32)
            a2rowb = lsb("a2rowb", [1, D], BF16)
            pb = lps("pb", [128, D], F32)
            for ic in range(NC):
                prog.dma("sync", S["sd"], wtmp[:, :],
                         dram["w1T"][ic * 128:(ic + 1) * 128, :])
                prog.wait("scalar", S["sd"])
                prog.inc("scalar", S["ss"], lambda e: e.sign(out=wsgb[:, :], in_=wtmp[:, :]))
                prog.wait("sync", S["ss"])
                prog.dma("sync", S["sd"], dram["scr_w1s"][ic * 128:(ic + 1) * 128, :],
                         wsgb[:, :])
                prog.wait("sync", S["sd"])
            for oc in range(32):
                prog.dma("sync", S["sd"], wtmp[:, 0:D],
                         dram["w1n"][oc * 128:(oc + 1) * 128, :])
                prog.wait("vector", S["sd"])
                prog.inc("vector", S["sv"], lambda e, oc=oc: e.reduce_sum(
                    sbuf["alpha1"][:, oc:oc + 1], wtmp[:, 0:D], axis=AX.X,
                    apply_absolute_value=True))
                prog.wait("sync", S["sv"])
            prog.step("vector", lambda e: e.tensor_scalar(
                out=sbuf["alpha1"][:, :], in0=sbuf["alpha1"][:, :], scalar1=1.0 / D,
                scalar2=None, op0=ALU.mult))
            for dc in range(NC):
                prog.dma("sync", S["sd"], wtmp[:, :],
                         dram["w2n"][dc * 128:(dc + 1) * 128, :])
                prog.wait("vector", S["sd"])
                prog.inc("vector", S["sv"], lambda e, dc=dc: e.reduce_sum(
                    a2pp[:, dc:dc + 1], wtmp[:, :], axis=AX.X, apply_absolute_value=True))
                prog.wait("sync", S["sv"])
            prog.step("vector", lambda e: e.tensor_scalar(
                out=a2pp[:, :], in0=a2pp[:, :], scalar1=1.0 / H0, scalar2=None,
                op0=ALU.mult))
            prog.wait("sync", S["sv"])
            for dc in range(NC):
                prog.dma("sync", S["sd"], a2row[0:1, dc * 128:(dc + 1) * 128],
                         a2pp[:, dc:dc + 1])
            prog.wait("sync", S["sd"])
            prog.wait("vector", S["sd"])
            prog.inc("vector", S["sv"], lambda e: e.tensor_copy(
                out=a2rowb[:, :], in_=a2row[:, :]))
            prog.wait("tensor", S["sv"])
            prog.mms(S["sp"], [
                lambda e: e.matmul(pb[:, 0:512], sbuf["ones1"][:, :],
                                   a2rowb[:, 0:512], start=True, stop=True),
                lambda e: e.matmul(pb[:, 512:D], sbuf["ones1"][:, :],
                                   a2rowb[:, 512:D], start=True, stop=True)])
            prog.wait("vector", S["sp"])
            for hc in range(32):
                prog.dma("sync", S["sd"], wtmp[:, 0:D],
                         dram["w2T"][hc * 128:(hc + 1) * 128, :])
                prog.wait("scalar", S["sd"])
                prog.inc("scalar", S["ss"], lambda e: e.sign(out=wsg[:, :], in_=wtmp[:, 0:D]))
                prog.wait("vector", S["ss"])
                prog.inc("vector", S["sv"], lambda e: e.tensor_tensor(
                    out=wsgb[:, 0:D], in0=wsg[:, :], in1=pb[:, :], op=ALU.mult))
                prog.wait("sync", S["sv"])
                prog.dma("sync", S["sd"], dram["scr_w2s"][hc * 128:(hc + 1) * 128, :],
                         wsgb[:, 0:D])
                prog.wait("sync", S["sd"])
            prog.flush()

    def emit_ffn0(first):
        if first:
            emit_ffn0_prep()
        emit_ln()
        with ExitStack() as lctx:
            lsb, lps = mk_local(lctx)
            w1b = [lsb(f"w1b{k}", [128, NC * 1024], BF16) for k in range(2)]
            w2b = [lsb(f"w2b{k}", [128, NC * 1024], BF16) for k in range(2)]
            hsb = [lsb(f"hsb{k}", [128, NC * 256], BF16) for k in range(2)]
            ph = [lps(f"ph{k}", [128, 256], F32) for k in range(2)]
            po = [lps(f"po{k}", [128, D], F32) for k in range(2)]
            xnT = sbuf["xnT"]
            ldone = [0, 0]

            def load_quarter(hq, buf):
                sem = S["sw%d" % buf]
                for ic in range(NC):
                    prog.dma("sync", sem, w1b[buf][:, ic * 1024:(ic + 1) * 1024],
                             dram["scr_w1s"][ic * 128:(ic + 1) * 128,
                                             hq * 1024:(hq + 1) * 1024])
                for hc in range(NC):
                    prog.dma("sync", sem, w2b[buf][:, hc * 1024:(hc + 1) * 1024],
                             dram["scr_w2s"][hq * 1024 + hc * 128:
                                             hq * 1024 + (hc + 1) * 128, :])
                ldone[buf] = sem.n

            load_quarter(0, 0)
            hs_free = [0, 0]
            ph_free = [0, 0]
            po_free = [0, 0]
            for hq in range(4):
                buf = hq % 2
                if hq < 3:
                    prog.wait("sync", S["sp"], S["sp"].n)
                    load_quarter(hq + 1, 1 - buf)
                prog.wait("tensor", S["sw%d" % buf], ldone[buf])
                for jj in range(NT // 2):
                    hset = jj % 2
                    if hs_free[hset]:
                        prog.wait("scalar", S["sp"], hs_free[hset])
                    for hc in range(NC):
                        bk = hc % 2
                        if ph_free[bk]:
                            prog.wait("tensor", S["ss"], ph_free[bk])
                        prog.mms(S["sp"], [
                            (lambda e, buf=buf, hc=hc, ic=ic, jj=jj, bk=bk: e.matmul(
                                ph[bk][:, :],
                                w1b[buf][:, ic * 1024 + hc * 128:ic * 1024 + (hc + 1) * 128],
                                xnT[:, ic * T + jj * 256:ic * T + (jj + 1) * 256],
                                start=(ic == 0), stop=(ic == NC - 1)))
                            for ic in range(NC)])
                        prog.wait("scalar", S["sp"], S["sp"].n)
                        prog.inc("scalar", S["ss"], lambda e, bk=bk, hset=hset, hc=hc, hq=hq:
                                 e.activation(out=hsb[hset][:, hc * 256:(hc + 1) * 256],
                                              in_=ph[bk][:, :], func=AF.Gelu,
                                              bias=sbuf["b1pp"][:, hq * 8 + hc:hq * 8 + hc + 1],
                                              scale=sbuf["alpha1"][:, hq * 8 + hc:hq * 8 + hc + 1]))
                        ph_free[bk] = S["ss"].n
                    prog.wait("tensor", S["ss"], S["ss"].n)
                    for ts in range(2):
                        j = jj * 2 + ts
                        pbk = ts
                        if po_free[pbk]:
                            prog.wait("tensor", S["sv"], po_free[pbk])
                        last_is_bias = (hq == 3)
                        fns = []
                        for hc in range(NC):
                            for dc2 in range(2):
                                stop = (hc == NC - 1 and not last_is_bias)
                                fns.append(lambda e, hset=hset, hc=hc, ts=ts, dc2=dc2,
                                           buf=buf, pbk=pbk, stop=stop: e.matmul(
                                               po[pbk][:, dc2 * 512:(dc2 + 1) * 512],
                                               hsb[hset][:, hc * 256 + ts * 128:
                                                         hc * 256 + (ts + 1) * 128],
                                               w2b[buf][:, hc * 1024 + dc2 * 512:
                                                        hc * 1024 + (dc2 + 1) * 512],
                                               start=(hc == 0), stop=stop))
                        if last_is_bias:
                            for dc2 in range(2):
                                fns.append(lambda e, pbk=pbk, dc2=dc2: e.matmul(
                                    po[pbk][:, dc2 * 512:(dc2 + 1) * 512],
                                    sbuf["ones1"][:, :],
                                    sbuf["b2rowb"][:, dc2 * 512:(dc2 + 1) * 512],
                                    start=False, stop=True))
                        prog.mms(S["sp"], fns)
                        prog.wait("vector", S["sp"], S["sp"].n)
                        prog.inc("vector", S["sv"], lambda e, j=j, pbk=pbk:
                                 e.scalar_tensor_tensor(
                                     out=sbuf["x"][:, j * D:(j + 1) * D], in0=po[pbk][:, :],
                                     scalar=C_HALF, in1=sbuf["x"][:, j * D:(j + 1) * D],
                                     op0=ALU.mult, op1=ALU.add))
                        po_free[pbk] = S["sv"].n
                    hs_free[hset] = S["sp"].n
            prog.flush()

    def emit_attn(layer):
        emit_ln()
        tile_sets = attn_tile_sets(layer)
        _, tile_mask = attn_mask_bank(layer)
        maskbank = dram["mask0" if layer == 0 else "mask1"]
        wq_s = dram["wqTs" if layer == 0 else "aqTs"]
        wk_s = dram["wkTs" if layer == 0 else "akTs"]
        wv_s = dram["wvTs" if layer == 0 else "avTs"]
        wo_in = dram["woT" if layer == 0 else "apT"]
        has_bias = (layer == 0)
        has_rope = (layer == 0)
        xnT = sbuf["xnT"]
        with ExitStack() as actx:
            asb, aps = mk_local(actx)
            nmask = attn_mask_bank(layer)[0].shape[0]
            masks = asb("masks", [128, nmask * 512], BF16)
            for mi in range(nmask):
                prog.dma("sync", S["sd"], masks[:, mi * 512:(mi + 1) * 512],
                         maskbank[mi, :, :])
            attnout = asb("attnout", [128, NC * T], BF16)
            hctx = ExitStack()
            ropeC = ropeS = None
            hsb, _ = mk_local(hctx)
            wsl = [[hsb(f"wsl{w}{k}", [128, NC * 128], BF16) for k in range(2)]
                   for w in range(3)]
            qraw = hsb("qraw", [128, T], BF16)
            kraw = hsb("kraw", [128, T], BF16)
            qsw = hsb("qsw", [128, T], BF16)
            qsq = hsb("qsq", [128, 512], BF16)
            qbuf = hsb("qbuf", [65, T], BF16)
            kbuf = hsb("kbuf", [65, T], BF16)
            v65 = hsb("v65", [128, NT * 130], BF16)
            nrm = hsb("nrm", [2, T], BF16)
            nscal = hsb("nscal", [2, 8], F32)
            if has_rope:
                ropeC = hsb("ropeC", [128, T], BF16)
                ropeS = hsb("ropeS", [128, T], BF16)
                prog.dma("sync", S["sd"], ropeC[:, :], dram["ropeCq"][:, :])
                prog.dma("sync", S["sd"], ropeS[:, :], dram["ropeSq"][:, :])
                prog.wait("sync", S["sd"])
                prog.wait("vector", S["sd"])
            pt = [hsb(f"pt{k}", [128, 1024], BF16) for k in range(2)]
            avsb = hsb("avsb", [64, 512], BF16)
            rrow = hsb("rrow", [1, 512], BF16)
            pq = [aps(f"pq{k}", [128, 512], F32) for k in range(2)]
            pn = aps("pn", [2, 512], F32)
            avps = aps("avps", [128, 512], F32)
            ps3 = [aps(f"ps3{k}", [128, 1024], F32) for k in range(2)]
            crow2 = qsw
            avsb2 = hsb("avsb2", [64, 512], BF16)
            ldone = [0, 0]

            def load_slices(hp, buf):
                sem = S["sg%d" % buf]
                for w, src in ((0, wq_s), (1, wk_s), (2, wv_s)):
                    prog.dma("gpsimd", sem, bass.AP(
                        wsl[w][buf], 0, [[NC * 128, 128], [128, NC], [1, 128]]),
                        bass.AP(src, hp * D * 128, [[128, 128], [128 * 128, NC], [1, 128]]))
                ldone[buf] = sem.n

            mask_ld = S["sd"].n
            prog.wait("sync", S["sd"], mask_ld)
            prog.wait("vector", S["sd"], mask_ld)
            avsb2_dma = [0]
            load_slices(0, 0)
            for hp in range(NC):
                buf = hp % 2
                if hp < NC - 1:
                    prog.wait("gpsimd", S["sp"], S["sp"].n)
                    load_slices(hp + 1, 1 - buf)
                prog.wait("tensor", S["sg%d" % buf], ldone[buf])
                prog.wait("tensor", S["sv"])
                prog.wait("tensor", S["ss"])
                prog.run("vector", lambda e: e.memset(nscal[0:2, 2:3], 0.0))
                pq_user_v = [0, 0]
                pn_user = 0
                pn_user_s = 0
                qsq_user = 0
                for wi, raw, brow in ((0, qraw, "bqrowb"),
                                      (1, kraw, "bkrowb")):
                    for tt in range(4):
                        bk = tt % 2
                        if pq_user_v[bk]:
                            prog.wait("tensor", S["sv"], pq_user_v[bk])
                        fns = []
                        for ic in range(NC):
                            last = (ic == NC - 1 and not has_bias)
                            fns.append(lambda e, wi=wi, buf=buf, ic=ic, tt=tt, bk=bk,
                                       last=last: e.matmul(
                                           pq[bk][:, :],
                                           wsl[wi][buf][:, ic * 128:(ic + 1) * 128],
                                           xnT[:, ic * T + tt * 512:ic * T + (tt + 1) * 512],
                                           start=(ic == 0), stop=last))
                        if has_bias:
                            fns.append(lambda e, bk=bk, brow=brow, hp=hp: e.matmul(
                                pq[bk][:, :],
                                sbuf[brow][0:1, hp * 128:(hp + 1) * 128],
                                sbuf["onesrow"][:, :], start=False, stop=True))
                        prog.mms(S["sp"], fns)
                        mmv = S["sp"].n
                        prog.wait("vector", S["sp"], mmv)
                        if wi == 0 and not has_rope:
                            prog.inc("vector", S["sv"], lambda e, bk=bk, raw=raw, tt=tt:
                                     e.tensor_scalar(out=raw[:, tt * 512:(tt + 1) * 512],
                                                     in0=pq[bk][:, :], scalar1=0.125,
                                                     scalar2=None, op0=ALU.mult))
                        else:
                            prog.inc("vector", S["sv"], lambda e, bk=bk, raw=raw, tt=tt:
                                     e.tensor_copy(out=raw[:, tt * 512:(tt + 1) * 512],
                                                   in_=pq[bk][:, :]))
                        pq_user_v[bk] = S["sv"].n
                        if "norm" not in ATTN_PARTS:
                            continue
                        if qsq_user:
                            prog.wait("vector", S["sp"], qsq_user)
                        prog.wait("vector", S["sv"])
                        prog.inc("vector", S["sv"], lambda e, raw=raw, tt=tt:
                                 e.tensor_mul(qsq[:, :],
                                              raw[:, tt * 512:(tt + 1) * 512],
                                              raw[:, tt * 512:(tt + 1) * 512]))
                        sqv = S["sv"].n
                        prog.wait("tensor", S["sv"], sqv)
                        if pn_user:
                            prog.wait("tensor", S["sv"], pn_user)
                        if pn_user_s:
                            prog.wait("tensor", S["ss"], pn_user_s)
                        prog.inc("tensor", S["sp"], lambda e: e.matmul(
                            pn[:, :], sbuf["ones2"][:, :], qsq[:, :],
                            start=True, stop=True))
                        qsq_user = S["sp"].n
                        if wi == 0:
                            prog.wait("scalar", S["sp"], S["sp"].n)
                            prog.inc("scalar", S["ss"], lambda e, tt=tt:
                                     e.activation(out=nrm[:, tt * 512:(tt + 1) * 512],
                                                  in_=pn[:, :], func=AF.Sqrt))
                            pn_user_s = S["ss"].n
                        else:
                            prog.wait("vector", S["sp"], S["sp"].n)
                            prog.step("vector", lambda e: e.reduce_max(
                                nscal[0:2, 3:4], pn[:, :], axis=AX.X))
                            prog.step("vector", lambda e: e.tensor_max(
                                nscal[0:2, 2:3], nscal[0:2, 2:3], nscal[0:2, 3:4]))
                            pn_user = S["sv"].n
                if "v" not in ATTN_PARTS:
                    continue
                prog.wait("tensor", S["ss"], S["ss"].n)
                prog.wait("tensor", S["sv"], S["sv"].n)
                v_evac = [0, 0]
                for jt in range(NT):
                    bk = jt % 2
                    dst = pq[bk][:, 0:128]
                    if v_evac[bk]:
                        prog.wait("tensor", S["sv"], v_evac[bk])
                    fns = []
                    for ic in range(NC):
                        last = (ic == NC - 1 and not has_bias)
                        fns.append(lambda e, buf=buf, ic=ic, jt=jt, dst=dst, last=last:
                                   e.matmul(dst,
                                            xnT[:, ic * T + jt * 128:ic * T + (jt + 1) * 128],
                                            wsl[2][buf][:, ic * 128:(ic + 1) * 128],
                                            start=(ic == 0), stop=last))
                    if has_bias:
                        fns.append(lambda e, dst=dst, hp=hp: e.matmul(
                            dst, sbuf["ones1"][:, :],
                            sbuf["bvrowb"][0:1, hp * 128:(hp + 1) * 128],
                            start=False, stop=True))
                    prog.mms(S["sp"], fns)
                    prog.wait("vector", S["sp"], S["sp"].n)
                    prog.inc("vector", S["sv"], lambda e, bk=bk, jt=jt: e.tensor_copy(
                        out=bass.AP(v65, jt * 130, [[NT * 130, 128], [65, 2], [1, 64]]),
                        in_=bass.AP(pq[bk], 0, [[512, 128], [64, 2], [1, 64]])))
                    v_evac[bk] = S["sv"].n
                prog.run("vector", lambda e: e.memset(
                    bass.AP(v65, 64, [[NT * 130, 128], [65, 2 * NT], [1, 1]]), 1.0))
                prog.inc("vector", S["sv"], lambda e: e.engine_nop())
                if ATTN_STOP <= 1:
                    continue
                if has_rope:
                    for raw in (qraw, kraw):
                        prog.wait("sync", S["sv"], S["sv"].n)
                        for h0 in (0, 64):
                            prog.dma("sync", S["sd"], qsw[h0:h0 + 32, :],
                                     raw[h0 + 32:h0 + 64, :])
                            prog.dma("sync", S["sd"], qsw[h0 + 32:h0 + 64, :],
                                     raw[h0:h0 + 32, :])
                        prog.wait("vector", S["sd"])
                        prog.step("vector", lambda e: e.tensor_tensor(
                            out=qsw[:, :], in0=qsw[:, :], in1=ropeS[:, :],
                            op=ALU.mult))
                        prog.step("vector", lambda e, raw=raw: e.tensor_tensor(
                            out=raw[:, :], in0=raw[:, :], in1=ropeC[:, :],
                            op=ALU.mult))
                        prog.step("vector", lambda e, raw=raw: e.tensor_add(
                            raw[:, :], raw[:, :], qsw[:, :]))
                    prog.step("vector", lambda e: e.tensor_scalar(
                        out=kraw[:, :], in0=kraw[:, :], scalar1=8.0, scalar2=None,
                        op0=ALU.mult))
                if ATTN_STOP <= 2 or (ATTN_STOP <= 3 and hp > 0):
                    continue
                prog.wait("scalar", S["sv"])
                prog.step("scalar", lambda e: e.activation(
                    out=nscal[0:2, 0:1], in_=nscal[0:2, 2:3], func=AF.Sqrt))
                prog.wait("vector", S["ss"])
                prog.step("vector", lambda e: e.tensor_scalar(
                    out=nscal[0:2, 1:2], in0=nscal[0:2, 0:1], scalar1=-0.125,
                    scalar2=None, op0=ALU.mult))
                prog.step("vector", lambda e: e.tensor_scalar(
                    out=crow2[0:2, 0:T], in0=nrm[0:2, 0:T],
                    scalar1=nscal[0:2, 1:2], scalar2=None, op0=ALU.mult))
                prog.run("vector", lambda e: e.memset(kbuf[64:65, :], 1.0))
                prog.inc("vector", S["sv"], lambda e: e.engine_nop())
                crow_v = S["sv"].n
                ps3_exp = [0, 0]
                pt_av = [0, 0]
                pt_msk = [0, 0]
                av_user = 0
                av_user_s = 0
                qk_dma = 0
                for sub in range(2):
                    head = 2 * hp + sub
                    prog.wait("sync", S["sv"], crow_v)
                    prog.wait("sync", S["sp"], S["sp"].n)
                    prog.dma("sync", S["sd"], kbuf[0:64, :],
                             kraw[sub * 64:(sub + 1) * 64, :])
                    prog.dma("sync", S["sd"], qbuf[0:64, :],
                             qraw[sub * 64:(sub + 1) * 64, :])
                    prog.dma("sync", S["sd"], qbuf[64:65, :], crow2[sub:sub + 1, :])
                    qk_dma = S["sd"].n
                    prog.wait("tensor", S["sd"], qk_dma)
                    for qt in range(4):
                        tiles = tile_sets[qt]
                        batches = [tiles[i:i + 2] for i in range(0, len(tiles), 2)]
                        first_av = True
                        av_emitted = 0
                        for bi, batch in enumerate(batches):
                            par = bi % 2
                            if ps3_exp[par]:
                                prog.wait("tensor", S["ss"], ps3_exp[par])
                            fns = []
                            for ti, (kt, _) in enumerate(batch):
                                fns.append(lambda e, par=par, ti=ti, kt=kt, qt=qt:
                                           e.matmul(ps3[par][:, ti * 512:(ti + 1) * 512],
                                                    kbuf[:, kt * 128:(kt + 1) * 128],
                                                    qbuf[:, qt * 512:(qt + 1) * 512],
                                                    start=True, stop=True))
                            prog.mms(S["sp"], fns)
                            sc_done = S["sp"].n
                            prog.wait("scalar", S["sp"], sc_done)
                            if pt_av[par]:
                                prog.wait("scalar", S["sp"], pt_av[par])
                                if pt_msk[par]:
                                    prog.wait("scalar", S["sv"], pt_msk[par])
                            nb = len(batch)
                            prog.inc("scalar", S["ss"], lambda e, par=par, nb=nb:
                                     e.activation(out=pt[par][:, 0:nb * 512],
                                                  in_=ps3[par][:, 0:nb * 512],
                                                  func=AF.Exp))
                            ev = S["ss"].n
                            ps3_exp[par] = ev
                            need_mask = [(ti, tile_mask[(qt, kt)])
                                         for ti, (kt, _) in enumerate(batch)
                                         if tile_mask[(qt, kt)] is not None]
                            if need_mask:
                                prog.wait("vector", S["ss"], ev)
                                for ti, mi in need_mask:
                                    prog.inc("vector", S["sv"],
                                             lambda e, par=par, ti=ti, mi=mi:
                                             e.tensor_tensor(
                                                 out=pt[par][:, ti * 512:(ti + 1) * 512],
                                                 in0=pt[par][:, ti * 512:(ti + 1) * 512],
                                                 in1=masks[:, mi * 512:(mi + 1) * 512],
                                                 op=ALU.mult))
                                pt_msk[par] = S["sv"].n
                                prog.wait("tensor", S["sv"], S["sv"].n)
                            else:
                                prog.wait("tensor", S["ss"], ev)
                            if first_av and (av_user or av_user_s):
                                prog.wait("tensor", S["sv"], av_user)
                                prog.wait("tensor", S["ss"], av_user_s)
                            fns = []
                            for ti, (kt, _) in enumerate(batch):
                                st = first_av and ti == 0
                                av_emitted += 1
                                fin = (av_emitted == len(tiles))
                                fns.append(lambda e, par=par, ti=ti, kt=kt, st=st,
                                           sub=sub, fin=fin: e.matmul(
                                               avps[0:65, :],
                                               bass.AP(v65, kt * 130 + sub * 65,
                                                       [[NT * 130, 128], [1, 65]]),
                                               pt[par][:, ti * 512:(ti + 1) * 512],
                                               start=st, stop=fin))
                            first_av = False
                            prog.mms(S["sp"], fns)
                            pt_av[par] = S["sp"].n
                        av_done = S["sp"].n
                        prog.wait("vector", S["sp"], av_done)

                        def _recip(e):
                            with nc.allow_low_precision("softmax 1/l in bf16"):
                                return e.reciprocal(rrow[:, :], avps[64:65, :])
                        prog.step("vector", _recip)
                        rv = S["sv"].n
                        prog.wait("tensor", S["sv"], rv)
                        prog.inc("tensor", S["sp"], lambda e: e.matmul(
                            pq[0][0:64, :], sbuf["ones1"][0:1, 0:64], rrow[:, :],
                            start=True, stop=True))
                        rbcv = S["sp"].n
                        prog.wait("scalar", S["sp"], av_done)
                        prog.wait("scalar", S["sv"], rv)
                        prog.inc("scalar", S["ss"], lambda e: e.copy(
                            avsb[:, :], avps[0:64, :]))
                        av_user_s = S["ss"].n
                        prog.wait("vector", S["ss"], av_user_s)
                        prog.wait("vector", S["sp"], rbcv)
                        if sub == 0:
                            prog.inc("vector", S["sv"], lambda e, hp=hp, qt=qt:
                                     e.tensor_tensor(
                                         out=attnout[0:64,
                                                     hp * T + qt * 512:hp * T + (qt + 1) * 512],
                                         in0=avsb[:, :], in1=pq[0][0:64, :], op=ALU.mult))
                        else:
                            prog.wait("vector", S["sab"], avsb2_dma[0])
                            prog.inc("vector", S["sv"], lambda e: e.tensor_tensor(
                                out=avsb2[:, :], in0=avsb[:, :], in1=pq[0][0:64, :],
                                op=ALU.mult))
                            prog.wait("sync", S["sv"], S["sv"].n)
                            avsb2_dma[0] = prog.dma(
                                "sync", S["sab"],
                                attnout[64:128, hp * T + qt * 512:hp * T + (qt + 1) * 512],
                                avsb2[:, :])
                        av_user = S["sv"].n
            if ATTN_STOP <= 4:
                hctx.close()
                prog.flush()
                return
            hctx.close()
            wob = asb("wob", [128, NC * 1024], BF16)
            prog.wait("gpsimd", S["sp"], S["sp"].n)
            prog.wait("gpsimd", S["ss"])
            prog.wait("gpsimd", S["sv"])
            for ic in range(NC):
                prog.dma("gpsimd", S["sg"], wob[:, ic * 1024:(ic + 1) * 1024],
                         wo_in[ic * 128:(ic + 1) * 128, :])
            prog.wait("tensor", S["sg"])
            prog.wait("tensor", S["sv"])
            prog.wait("tensor", S["ss"])
            prog.wait("tensor", S["sd"])
            prog.wait("tensor", S["sab"])
            po = ps3
            po_free = [0, 0]
            for j in range(NT):
                pbk = j % 2
                if po_free[pbk]:
                    prog.wait("tensor", S["sv"], po_free[pbk])
                fns = []
                for ic in range(NC):
                    last = (ic == NC - 1 and not has_bias)
                    for dc2 in range(2):
                        fns.append(lambda e, ic=ic, j=j, dc2=dc2, pbk=pbk,
                                   last=last: e.matmul(
                                       po[pbk][:, dc2 * 512:(dc2 + 1) * 512],
                                       attnout[:, ic * T + j * 128:ic * T + (j + 1) * 128],
                                       wob[:, ic * 1024 + dc2 * 512:ic * 1024 + (dc2 + 1) * 512],
                                       start=(ic == 0), stop=last))
                if has_bias:
                    for dc2 in range(2):
                        fns.append(lambda e, pbk=pbk, dc2=dc2: e.matmul(
                            po[pbk][:, dc2 * 512:(dc2 + 1) * 512],
                            sbuf["ones1"][:, :],
                            sbuf["borowb"][:, dc2 * 512:(dc2 + 1) * 512],
                            start=False, stop=True))
                prog.mms(S["sp"], fns)
                prog.wait("vector", S["sp"], S["sp"].n)
                prog.inc("vector", S["sv"], lambda e, j=j, pbk=pbk: e.tensor_add(
                    sbuf["x"][:, j * D:(j + 1) * D], po[pbk][:, :],
                    sbuf["x"][:, j * D:(j + 1) * D]))
                po_free[pbk] = S["sv"].n
            prog.flush()

    PIECES = [(0, 6), (6, 6), (12, 5), (17, 5)]

    def _ffn1_commons():
        new = "ones1f" not in sbuf
        sb("sc_misc", [128, 8], F32)
        sb("grow", [1, 128], F32)
        sb("ccs", [1, 16], F32)
        sb("ones1f", [1, 128], F32)
        if new:
            prog.dma("sync", S["sd"], sbuf["ones1f"][:, :], dram["ones1f"][:, :])
            prog.wait("sync", S["sd"])
            prog.wait("tensor", S["sd"])

    def bcast_pp(dst_sb, src_11, psum_t):
        prog.wait("tensor", S["sv"])
        prog.inc("tensor", S["sp"], lambda e: e.matmul(
            psum_t[:, 0:1], sbuf["ones1f"][:, :], src_11, start=True, stop=True))
        prog.wait("vector", S["sp"])
        prog.step("vector", lambda e: e.tensor_copy(out=dst_sb, in_=psum_t[:, 0:1]))

    def emit_allreduce_max(src_pp, dst_11):
        prog.wait("sync", S["sv"])
        prog.dma("sync", S["sd"], sbuf["grow"][0:1, 0:128], src_pp)
        prog.wait("vector", S["sd"])
        prog.step("vector", lambda e: e.reduce_max(
            sbuf["scal"][0:1, 0:1], sbuf["grow"][0:1, 0:128], axis=AX.X))
        prog.step("vector", lambda e: e.memset(sbuf["ccs"][:, :], 0.0))
        prog.step("vector", lambda e: e.tensor_copy(
            out=sbuf["ccs"][0:1, 0:1], in_=sbuf["scal"][0:1, 0:1]))
        prog.wait("sync", S["sv"])
        prog.dma("sync", S["sd"], dram["cc_in"][:, :], sbuf["ccs"][:, :])
        prog.wait("gpsimd", S["sd"])
        prog.inc("gpsimd", S["scc"], lambda e: e.collective_compute(
            "AllReduce", ALU.max, replica_groups=[list(range(8))],
            ins=[dram["cc_in"].ap()], outs=[dram["cc_out"].ap()]))
        prog.wait("sync", S["scc"])
        prog.dma("sync", S["sd"], sbuf["ccs"][:, :], dram["cc_out"][:, :])
        prog.wait("vector", S["sd"])
        prog.step("vector", lambda e: e.tensor_copy(out=dst_11, in_=sbuf["ccs"][0:1, 0:1]))

    def emit_ffn1_prep():
        with ExitStack() as lctx:
            lsb, lps = mk_local(lctx)
            wtmp = lsb("wtmp", [128, I1], F32)
            wint = lsb("wint", [128, 1024], I32)
            wbf = lsb("wbf", [128, 1024], BF16)
            wtr = lsb("wtr", [128, 1024], BF16)
            ptr = lps("ptr", [128, 1024], BF16)
            for wn, col0 in (("wgn", 0), ("wun", 22)):
                for oc in range(NI1):
                    prog.dma("sync", S["sd"], wtmp[:, 0:D],
                             dram[wn][oc * 128:(oc + 1) * 128, :])
                    prog.wait("vector", S["sd"])
                    prog.inc("vector", S["sv"], lambda e, oc=oc, col0=col0: e.reduce_max(
                        sbuf["sgu_sc"][:, col0 + oc:col0 + oc + 1], wtmp[:, 0:D],
                        axis=AX.X, apply_absolute_value=True))
                    prog.wait("sync", S["sv"])
            for dc in range(NC):
                prog.dma("sync", S["sd"], wtmp[:, :],
                         dram["wdn"][dc * 128:(dc + 1) * 128, :])
                prog.wait("vector", S["sd"])
                prog.inc("vector", S["sv"], lambda e, dc=dc: e.reduce_max(
                    sbuf["sdrow"][:, dc:dc + 1], wtmp[:, :], axis=AX.X,
                    apply_absolute_value=True))
                prog.wait("sync", S["sv"])
            prog.wait("vector", S["sv"])
            prog.step("vector", lambda e: e.tensor_scalar(
                out=sbuf["sgu_sc"][:, :], in0=sbuf["sgu_sc"][:, :], scalar1=1.0 / 127.0,
                scalar2=None, op0=ALU.mult))
            prog.step("vector", lambda e: e.tensor_scalar(
                out=sbuf["sdrow"][:, :], in0=sbuf["sdrow"][:, :], scalar1=1.0 / 127.0,
                scalar2=None, op0=ALU.mult))
            prog.step("vector", lambda e: e.reciprocal(
                sbuf["sgx"][:, :], sbuf["sgu_sc"][:, :]))
            prog.step("vector", lambda e: e.reciprocal(
                sbuf["sc_misc"][:, 0:8], sbuf["sdrow"][:, :]))
            prog.flush()
            for wn, scr, col0 in (("wgn", "scr_g", 0), ("wun", "scr_u", 22)):
                for oc in range(NI1):
                    prog.wait("sync", S["sp"], S["sp"].n)
                    prog.dma("sync", S["sd"], wtmp[:, 0:D],
                             dram[wn][oc * 128:(oc + 1) * 128, :])
                    prog.wait("vector", S["sd"])
                    prog.step("vector", lambda e, oc=oc, col0=col0: e.tensor_scalar(
                        out=wint[:, :], in0=wtmp[:, 0:D],
                        scalar1=sbuf["sgx"][:, col0 + oc:col0 + oc + 1],
                        scalar2=None, op0=ALU.mult))
                    prog.step("vector", lambda e: e.tensor_scalar(
                        out=wbf[:, :], in0=wint[:, :], scalar1=-128.0, scalar2=127.0,
                        op0=ALU.max, op1=ALU.min))
                    prog.wait("tensor", S["sv"])
                    prog.mms(S["sp"], [
                        (lambda e, c=c: e.transpose(
                            ptr[:, c * 128:(c + 1) * 128], wbf[:, c * 128:(c + 1) * 128],
                            sbuf["ident"][:, :])) for c in range(NC)])
                    prog.wait("vector", S["sp"])
                    prog.wait("vector", S["sd"])
                    prog.inc("vector", S["sv"], lambda e: e.tensor_copy(
                        out=wtr[:, :], in_=ptr[:, :]))
                    prog.wait("sync", S["sv"])
                    prog.dma("sync", S["sd"], bass.AP(
                        dram[scr], oc * 128, [[I1, 128], [I1 * 128, NC], [1, 128]]),
                        bass.AP(wtr, 0, [[1024, 128], [128, NC], [1, 128]]))
                    prog.wait("sync", S["sd"])
            for dc in range(NC):
                prog.wait("sync", S["sp"], S["sp"].n)
                prog.dma("sync", S["sd"], wtmp[:, :],
                         dram["wdn"][dc * 128:(dc + 1) * 128, :])
                prog.wait("vector", S["sd"])
                for s0, sl in ((0, 1024), (1024, 1024), (2048, 768)):
                    prog.step("vector", lambda e, dc=dc, s0=s0, sl=sl: e.tensor_scalar(
                        out=wint[:, 0:sl], in0=wtmp[:, s0:s0 + sl],
                        scalar1=sbuf["sc_misc"][:, dc:dc + 1], scalar2=None,
                        op0=ALU.mult))
                    prog.step("vector", lambda e, sl=sl: e.tensor_scalar(
                        out=wbf[:, 0:sl], in0=wint[:, 0:sl], scalar1=-128.0,
                        scalar2=127.0, op0=ALU.max, op1=ALU.min))
                    prog.step("vector", lambda e, dc=dc, sl=sl: e.tensor_scalar(
                        out=wbf[:, 0:sl], in0=wbf[:, 0:sl],
                        scalar1=sbuf["sdrow"][:, dc:dc + 1], scalar2=None,
                        op0=ALU.mult))
                    nctr = sl // 128
                    prog.wait("tensor", S["sv"])
                    prog.mms(S["sp"], [
                        (lambda e, c=c: e.transpose(
                            ptr[:, c * 128:(c + 1) * 128], wbf[:, c * 128:(c + 1) * 128],
                            sbuf["ident"][:, :])) for c in range(nctr)])
                    prog.wait("vector", S["sp"])
                    prog.wait("vector", S["sd"])
                    prog.inc("vector", S["sv"], lambda e, nctr=nctr: e.tensor_copy(
                        out=wtr[:, 0:nctr * 128], in_=ptr[:, 0:nctr * 128]))
                    prog.wait("sync", S["sv"])
                    prog.dma("sync", S["sd"], bass.AP(
                        dram["scr_d"], (s0 + 0) * D + dc * 128,
                        [[D, 128], [D * 128, nctr], [1, 128]]),
                        bass.AP(wtr, 0, [[1024, 128], [128, nctr], [1, 128]]))
                    prog.wait("sync", S["sd"])
            prog.dma("sync", S["sd"], sbuf["scal"][0:1, 2:4], dram["clips"][:, :])
            prog.wait("scalar", S["sd"])
            prog.inc("scalar", S["ss"], lambda e: e.activation(
                out=sbuf["scal"][0:1, 4:6], in_=sbuf["scal"][0:1, 2:4], func=AF.Sigmoid))
            prog.flush()

    def emit_ffn1(first, store=False):
        _ffn1_commons()
        if first:
            emit_ffn1_prep()
        emit_ln(fp32_scratch=True)
        xq = sbuf["xnT"]
        with ExitStack() as lctx:
            lsb, lps = mk_local(lctx)
            xf = lsb("xf", [128, T], F32)
            xi = lsb("xi", [128, 1024], I32)
            gbuf = lsb("gbuf", [128, NC * 768], BF16)
            ubuf = lsb("ubuf", [128, NC * 768], BF16)
            dbuf = lsb("dbuf", [128, 6 * 1024], BF16)
            gs = [lsb(f"gs{k}", [128, 512], BF16) for k in range(2)]
            us = [lsb(f"us{k}", [128, 512], BF16) for k in range(2)]
            hs = [lsb(f"hs{k}", [128, 512], BF16) for k in range(2)]
            hq = [lsb(f"hq{k}", [128, 6 * 512], BF16) for k in range(2)]
            hmax = lsb("hmax", [128, 2], F32)
            sxb = lsb("sxb", [128, 1], F32)
            shb = lsb("shb", [128, 1], F32)
            sh2 = lsb("sh2", [128, 1], F32)
            pg = [lps(f"pg{k}", [128, 512], F32) for k in range(2)]
            pu = [lps(f"pu{k}", [128, 512], F32) for k in range(2)]
            pd = [lps(f"pd{k}", [128, D], F32) for k in range(2)]
            emit_allreduce_max(sbuf["st_t"][:, 0:1], sbuf["scal"][0:1, 6:7])
            prog.step("vector", lambda e: e.tensor_mul(
                sbuf["scal"][0:1, 7:8], sbuf["scal"][0:1, 6:7], sbuf["scal"][0:1, 4:5]))
            prog.step("vector", lambda e: e.tensor_scalar(
                out=sbuf["scal"][0:1, 7:8], in0=sbuf["scal"][0:1, 7:8],
                scalar1=1.0 / 127.0, scalar2=None, op0=ALU.mult))
            prog.step("vector", lambda e: e.reciprocal(
                sbuf["scal"][0:1, 8:9], sbuf["scal"][0:1, 7:8]))
            bcast_pp(sxb[:, :], sbuf["scal"][0:1, 8:9], pg[0])
            sxv = lsb("sxv", [128, 1], F32)
            bcast_pp(sxv[:, :], sbuf["scal"][0:1, 7:8], pg[0])
            prog.step("vector", lambda e: e.tensor_scalar(
                out=sbuf["sgx"][:, :], in0=sbuf["sgu_sc"][:, :],
                scalar1=sxv[:, :], scalar2=None, op0=ALU.mult))
            prog.flush()
            prog.wait("sync", S["sxf"])
            xq_dma = 0
            for ic in range(NC):
                if xq_dma:
                    prog.wait("sync", S["sv"])
                prog.dma("sync", S["sxf"], xf[:, :],
                         dram["scr_xnT"][ic * 128:(ic + 1) * 128, :])
                xq_dma = S["sxf"].n
                prog.wait("vector", S["sxf"], xq_dma)
                for h2 in range(2):
                    prog.step("vector", lambda e, h2=h2: e.tensor_scalar(
                        out=xi[:, :], in0=xf[:, h2 * 1024:(h2 + 1) * 1024],
                        scalar1=sxb[:, :], scalar2=None, op0=ALU.mult))
                    prog.step("vector", lambda e, ic=ic, h2=h2: e.tensor_scalar(
                        out=xq[:, ic * T + h2 * 1024:ic * T + (h2 + 1) * 1024],
                        in0=xi[:, :], scalar1=-128.0, scalar2=127.0,
                        op0=ALU.max, op1=ALU.min))
            prog.run("vector", lambda e: e.memset(hmax[:, :], 0.0))
            prog.flush()
            ldw = 0
            pg_free = [0, 0]
            pu_free = [0, 0]
            gs_user = [0, 0]
            for pi, (oc0, nch) in enumerate(PIECES):
                prog.wait("sync", S["sp"], S["sp"].n)
                for ic in range(NC):
                    prog.dma("sync", S["sw0"], gbuf[:, ic * 768:ic * 768 + nch * 128],
                             dram["scr_g"][ic * 128:(ic + 1) * 128,
                                           oc0 * 128:(oc0 + nch) * 128])
                    prog.dma("sync", S["sw0"], ubuf[:, ic * 768:ic * 768 + nch * 128],
                             dram["scr_u"][ic * 128:(ic + 1) * 128,
                                           oc0 * 128:(oc0 + nch) * 128])
                ldw = S["sw0"].n
                prog.wait("tensor", S["sw0"], ldw)
                for ci in range(nch):
                    oc = oc0 + ci
                    for tt in range(4):
                        bk = tt % 2
                        if pg_free[bk]:
                            prog.wait("tensor", S["ss"], pg_free[bk])
                            prog.wait("tensor", S["sv"], pu_free[bk])
                        for which, (buf, ps_, ev) in enumerate(
                                ((gbuf, pg, "g"), (ubuf, pu, "u"))):
                            fns = [(lambda e, buf=buf, ps_=ps_, bk=bk, ci=ci, ic=ic,
                                    tt=tt: e.matmul(
                                        ps_[bk][:, :],
                                        buf[:, ic * 768 + ci * 128:ic * 768 + (ci + 1) * 128],
                                        xq[:, ic * T + tt * 512:ic * T + (tt + 1) * 512],
                                        start=(ic == 0), stop=(ic == NC - 1)))
                                   for ic in range(NC)]
                            prog.mms(S["sp"], fns)
                        mmv = S["sp"].n
                        prog.wait("scalar", S["sp"], mmv)
                        prog.wait("scalar", S["ss"])
                        if gs_user[bk]:
                            prog.wait("scalar", S["sv"], gs_user[bk])
                        prog.inc("scalar", S["ss"], lambda e, bk=bk, oc=oc: e.activation(
                            out=gs[bk][:, :], in_=pg[bk][:, :], func=AF.Silu,
                            scale=sbuf["sgx"][:, oc:oc + 1]))
                        sev = S["ss"].n
                        pg_free[bk] = sev
                        prog.wait("vector", S["sp"], mmv)
                        prog.inc("vector", S["sv"], lambda e, bk=bk, oc=oc: e.tensor_scalar(
                            out=us[bk][:, :], in0=pu[bk][:, :],
                            scalar1=sbuf["sgx"][:, 22 + oc:22 + oc + 1], scalar2=None,
                            op0=ALU.mult))
                        pu_free[bk] = S["sv"].n
                        prog.wait("vector", S["ss"], sev)
                        prog.wait("vector", S["sv"])
                        prog.wait("vector", S["sh%d" % bk])
                        prog.inc("vector", S["sv"], lambda e, bk=bk: e.tensor_mul(
                            hs[bk][:, :], gs[bk][:, :], us[bk][:, :]))
                        gs_user[bk] = S["sv"].n
                        prog.wait("vector", S["sv"])
                        prog.step("vector", lambda e, bk=bk: e.reduce_max(
                            hmax[:, 1:2], hs[bk][:, :], axis=AX.X,
                            apply_absolute_value=True))
                        prog.step("vector", lambda e: e.tensor_max(
                            hmax[:, 0:1], hmax[:, 0:1], hmax[:, 1:2]))
                        prog.wait("sync", S["sv"])
                        prog.dma("sync", S["sh%d" % bk], dram["scr_h"][
                            oc * 128:(oc + 1) * 128, tt * 512:(tt + 1) * 512],
                            hs[bk][:, :])
            prog.flush()
            emit_allreduce_max(hmax[:, 0:1], sbuf["scal"][0:1, 9:10])
            prog.step("vector", lambda e: e.tensor_mul(
                sbuf["scal"][0:1, 10:11], sbuf["scal"][0:1, 9:10], sbuf["scal"][0:1, 5:6]))
            prog.step("vector", lambda e: e.tensor_scalar(
                out=sbuf["scal"][0:1, 10:11], in0=sbuf["scal"][0:1, 10:11],
                scalar1=1.0 / 127.0, scalar2=None, op0=ALU.mult))
            prog.step("vector", lambda e: e.reciprocal(
                sbuf["scal"][0:1, 11:12], sbuf["scal"][0:1, 10:11]))
            prog.step("vector", lambda e: e.tensor_scalar(
                out=sbuf["scal"][0:1, 12:13], in0=sbuf["scal"][0:1, 10:11],
                scalar1=0.5, scalar2=None, op0=ALU.mult))
            bcast_pp(shb[:, :], sbuf["scal"][0:1, 11:12], pg[0])
            bcast_pp(sh2[:, :], sbuf["scal"][0:1, 12:13], pg[0])
            prog.flush()
            prog.wait("sync", S["sh0"])
            prog.wait("sync", S["sh1"])
            for pi, (oc0, nch) in enumerate(PIECES):
                prog.wait("sync", S["sp"], S["sp"].n)
                for ci in range(nch):
                    prog.dma("sync", S["sw1"], dbuf[:, ci * 1024:(ci + 1) * 1024],
                             dram["scr_d"][(oc0 + ci) * 128:(oc0 + ci + 1) * 128, :])
                ldw = S["sw1"].n
                prog.wait("tensor", S["sw1"], ldw)
                hq_dma = [0, 0]
                hq_used = [0, 0]
                pd_free = [0, 0]
                for tg in range(4):
                    hb = tg % 2
                    if hq_used[hb]:
                        prog.wait("sync", S["sp"], hq_used[hb])
                    for ci in range(nch):
                        prog.dma("sync", S["sh%d" % hb], hq[hb][:, ci * 512:(ci + 1) * 512],
                                 dram["scr_h"][(oc0 + ci) * 128:(oc0 + ci + 1) * 128,
                                               tg * 512:(tg + 1) * 512])
                    hqd = S["sh%d" % hb].n
                    prog.wait("vector", S["sh%d" % hb], hqd)
                    for ci in range(nch):
                        prog.step("vector", lambda e, hb=hb, ci=ci: e.tensor_scalar(
                            out=xi[:, 0:512], in0=hq[hb][:, ci * 512:(ci + 1) * 512],
                            scalar1=shb[:, :], scalar2=None, op0=ALU.mult))
                        prog.step("vector", lambda e, hb=hb, ci=ci: e.tensor_scalar(
                            out=hq[hb][:, ci * 512:(ci + 1) * 512], in0=xi[:, 0:512],
                            scalar1=-128.0, scalar2=127.0, op0=ALU.max, op1=ALU.min))
                    hq_ready = S["sv"].n
                    prog.wait("tensor", S["sv"], hq_ready)
                    for ts in range(4):
                        j = tg * 4 + ts
                        pbk = ts % 2
                        if pd_free[pbk]:
                            prog.wait("tensor", S["sv"], pd_free[pbk])
                        fns = []
                        for ci in range(nch):
                            for dc2 in range(2):
                                fns.append(lambda e, hb=hb, ci=ci, ts=ts, dc2=dc2,
                                           pbk=pbk, nch=nch: e.matmul(
                                               pd[pbk][:, dc2 * 512:(dc2 + 1) * 512],
                                               hq[hb][:, ci * 512 + ts * 128:
                                                      ci * 512 + (ts + 1) * 128],
                                               dbuf[:, ci * 1024 + dc2 * 512:
                                                    ci * 1024 + (dc2 + 1) * 512],
                                               start=(ci == 0), stop=(ci == nch - 1)))
                        prog.mms(S["sp"], fns)
                        prog.wait("vector", S["sp"], S["sp"].n)
                        prog.inc("vector", S["sv"], lambda e, j=j, pbk=pbk:
                                 e.scalar_tensor_tensor(
                                     out=sbuf["x"][:, j * D:(j + 1) * D],
                                     in0=pd[pbk][:, :], scalar=sh2[:, :],
                                     in1=sbuf["x"][:, j * D:(j + 1) * D],
                                     op0=ALU.mult, op1=ALU.add))
                        pd_free[pbk] = S["sv"].n
                        if store and pi == len(PIECES) - 1:
                            prog.wait("sync", S["sv"], pd_free[pbk])
                            prog.dma("sync", S["so"],
                                     dram["y"][j * 128:(j + 1) * 128, :],
                                     sbuf["x"][:, j * D:(j + 1) * D])
                    hq_used[hb] = S["sp"].n
            prog.flush()

    emit_ffn0(True)
    if n_sub >= 2:
        emit_attn(0)
    if n_sub >= 3:
        emit_ffn0(False)
    if n_sub >= 4:
        emit_ffn1(True)
    if n_sub >= 5:
        emit_attn(1)
    if n_sub >= 6:
        emit_ffn1(False, store=True)

    if n_sub < 6:
        for j in range(NT):
            prog.dma("sync", S["so"], dram["y"][j * 128:(j + 1) * 128, :],
                     sbuf["x"][:, j * D:(j + 1) * D])
    prog.wait("sync", S["so"])
    prog.flush()

    ctx.close()
    return nc



def _host_consts():
    c = {}
    c["identity"] = np.eye(128, dtype=BF)
    c["identityf"] = np.eye(128, dtype=np.float32)
    j = np.arange(32).astype(np.float64)
    inv = 1.0 / 10000.0 ** (2.0 * j / 64.0)
    pos = np.arange(T, dtype=np.float64)
    ang = pos[None, :] * inv[:, None]
    cos = np.cos(ang); sin = np.sin(ang)
    Chead = np.concatenate([cos, cos], axis=0)
    Shead = np.concatenate([-sin, sin], axis=0)
    Cfull = np.tile(Chead, (2, 1))
    Sfull = np.tile(Shead, (2, 1))
    c["ropeCq"] = (Cfull / 8.0).astype(BF)
    c["ropeSq"] = (Sfull / 8.0).astype(BF)
    c["mask0"] = attn_mask_bank(0)[0]
    c["mask1"] = attn_mask_bank(1)[0]
    cf = np.zeros((128, 8), dtype=np.float32)
    cf[:, 0] = 0.5; cf[:, 1] = 1.0; cf[:, 2] = -1.0; cf[:, 3] = 1.0 / D
    c["constf"] = cf
    c["ones1"] = np.ones((1, 128), dtype=BF)
    c["ones1f"] = np.ones((1, 128), dtype=np.float32)
    c["onesrow"] = np.ones((1, 512), dtype=BF)
    o2 = np.zeros((128, 2), dtype=BF)
    o2[:64, 0] = 1; o2[64:, 1] = 1
    c["ones2"] = o2
    return c


def _host_inputs(inp):
    tr = lambda a: np.ascontiguousarray(np.asarray(a, dtype=np.float32).T)
    cp = lambda a: np.ascontiguousarray(np.asarray(a, dtype=np.float32))
    g = {}
    g["w1T"] = tr(inp["L0_ffn_w1"]); g["w1n"] = cp(inp["L0_ffn_w1"])
    g["w2T"] = tr(inp["L0_ffn_w2"]); g["w2n"] = cp(inp["L0_ffn_w2"])
    g["b1pp"] = cp(np.asarray(inp["L0_ffn_b1"]).reshape(32, 128).T)
    g["b2row"] = cp(np.asarray(inp["L0_ffn_b2"]).reshape(1, D))
    g["wqT"] = cp(inp["L0_attn_wq"]); g["wkT"] = cp(inp["L0_attn_wk"])
    g["wvT"] = cp(inp["L0_attn_wv"]); g["woT"] = cp(inp["L0_attn_wo"])
    g["bqrow"] = cp(np.asarray(inp["L0_attn_bq"]).reshape(1, D))
    g["bkrow"] = cp(np.asarray(inp["L0_attn_bk"]).reshape(1, D))
    g["bvrow"] = cp(np.asarray(inp["L0_attn_bv"]).reshape(1, D))
    g["borow"] = cp(np.asarray(inp["L0_attn_bo"]).reshape(1, D))
    wa = np.asarray(inp["L1_attn_w_attn"], dtype=np.float32)
    g["aqT"] = tr(wa[0:D]); g["akT"] = tr(wa[D:2 * D]); g["avT"] = tr(wa[2 * D:3 * D])
    g["apT"] = tr(inp["L1_attn_w_proj"])
    g["wgT"] = tr(inp["L1_ffn_w_gate"]); g["wgn"] = cp(inp["L1_ffn_w_gate"])
    g["wuT"] = tr(inp["L1_ffn_w_up"]); g["wun"] = cp(inp["L1_ffn_w_up"])
    g["wdT"] = tr(inp["L1_ffn_w_down"]); g["wdn"] = cp(inp["L1_ffn_w_down"])
    perm64 = np.concatenate([np.arange(0, 64, 2), np.arange(1, 64, 2)])
    permD = (np.arange(D).reshape(16, 64)[:, perm64]).ravel()
    for nm in ("wqT", "wkT"):
        g[nm] = np.ascontiguousarray(g[nm][:, permD])
    g["bqrow"] = np.ascontiguousarray(g["bqrow"][:, permD])
    g["bkrow"] = np.ascontiguousarray(g["bkrow"][:, permD])
    for nm in ("wqT", "wkT", "wvT", "aqT", "akT", "avT"):
        g[nm + "s"] = np.ascontiguousarray(
            g[nm].reshape(D, 8, 128).transpose(1, 0, 2))
    g["clips"] = np.array([[float(np.asarray(inp["L1_ffn_clip_x"]).ravel()[0]),
                            float(np.asarray(inp["L1_ffn_clip_down"]).ravel()[0])]],
                          dtype=np.float32)
    return g


def kernel(**inputs):
    x = np.asarray(inputs["L0_attn_x"], dtype=np.float32)
    shared = _host_inputs(inputs)
    shared.update(_host_consts())
    nc = build_program(N_SUB)
    in_maps = []
    for core in range(8):
        m = dict(shared)
        m["x_in"] = np.ascontiguousarray(x[core])
        in_maps.append(m)
    res = run_bass_kernel_spmd(nc, in_maps, core_ids=list(range(8)))
    out = np.stack([np.asarray(r["y"]) for r in res.results], axis=0)
    return out.astype(np.float32)


# version 04 (final)
# speedup vs baseline: 1.0954x; 1.0954x over previous
from contextlib import ExitStack

import numpy as np
import ml_dtypes
import concourse.bass as bass
import concourse.mybir as mybir
from concourse.bass_utils import run_bass_kernel_spmd

F32 = mybir.dt.float32
BF16 = mybir.dt.bfloat16
I32 = mybir.dt.int32
AF = mybir.ActivationFunctionType
ALU = mybir.AluOpType
AX = mybir.AxisListType
BF = ml_dtypes.bfloat16

T = 2048
D = 1024
NT = 16
NC = 8
H0 = 4096
I1 = 2816
NI1 = 22

N_SUB = 6
DEBUG_TAPS = False
ATTN_STOP = 99
ATTN_PARTS = {"qk", "norm", "v"}

ENGINES = ("tensor", "scalar", "vector", "gpsimd", "sync")


def attn_tile_sets(layer):
    p = np.arange(128)[:, None]
    f = np.arange(512)[None, :]
    out = []
    for qt in range(4):
        row = []
        if layer == 0:
            kts = range(0, 4 * qt + 4)
        else:
            w = [kt for kt in range(max(0, 4 * qt - 2), 4 * qt + 4)]
            kts = sorted(set(([0] if 0 not in w else []) + w))
        for kt in kts:
            k = kt * 128 + p
            q = qt * 512 + f
            if layer == 0:
                valid = (k <= q)
            else:
                valid = (k <= q) & ((k >= q - 255) | (k <= 3))
            if valid.all():
                row.append((kt, None))
            else:
                row.append((kt, valid.astype(np.uint8).tobytes()))
        out.append(row)
    return out


def attn_mask_bank(layer):
    sets_ = attn_tile_sets(layer)
    patterns, index = [], {}
    for qt, row in enumerate(sets_):
        for kt, mb in row:
            if mb is None:
                continue
            if mb not in index:
                index[mb] = len(patterns)
                patterns.append(np.frombuffer(mb, dtype=np.uint8).reshape(128, 512))
    bank = np.stack(patterns) if patterns else np.zeros((1, 128, 512), np.uint8)
    tile_mask = {}
    for qt, row in enumerate(sets_):
        for kt, mb in row:
            tile_mask[(qt, kt)] = None if mb is None else index[mb]
    return bank.astype(BF), tile_mask


class Sem:
    def __init__(self, nc, ctx, name):
        self.h = ctx.enter_context(nc.semaphore(name))
        self.n = 0


class Prog:

    def __init__(self, nc, ctx):
        self.nc = nc
        self.ops = {e: [] for e in ENGINES}
        self.sems = {n: Sem(nc, ctx, n) for n in
                     ("sd", "sp", "ss", "sv", "sg", "so", "sw0", "sw1", "sab", "sxf", "sg0", "sg1", "scc", "sh0", "sh1")}
        self.audit = {e: [] for e in ENGINES}
        self.nblocks = 0

    def emit(self, e, fn):
        self.ops[e].append(fn)

    def wait(self, e, sem, val=None):
        v = sem.n if val is None else val
        if v > 0:
            self.audit[e].append(("wait", sem, v))
            self.emit(e, lambda eng, s=sem.h, vv=v: eng.wait_ge(s, vv))

    def dma(self, e, sem, out, in_):
        sem.n += 16
        self.audit[e].append(("inc", sem, 16))
        self.emit(e, lambda eng, o=out, i=in_, s=sem.h:
                  eng.dma_start(out=o, in_=i).then_inc(s, 16))
        return sem.n

    def inc(self, e, sem, fn):
        sem.n += 1
        self.audit[e].append(("inc", sem, 1))
        self.emit(e, lambda eng, f=fn, s=sem.h: f(eng).then_inc(s, 1))
        return sem.n

    def run(self, e, fn):
        self.emit(e, fn)

    def step(self, e, fn):
        m = {"vector": "sv", "scalar": "ss", "gpsimd": "sg", "tensor": "sp"}[e]
        sm = self.sems[m]
        self.wait(e, sm, sm.n)
        self.inc(e, sm, fn)
        self.wait(e, sm, sm.n)

    def check_deadlock(self):
        pos = {e: 0 for e in ENGINES}
        cnt = {id(s): 0 for s in self.sems.values()}
        logs = self.audit
        while True:
            prog_found = False
            for e in ENGINES:
                while pos[e] < len(logs[e]):
                    kind, sem, val = logs[e][pos[e]]
                    if kind == "inc":
                        cnt[id(sem)] += val
                        pos[e] += 1
                        prog_found = True
                    elif kind == "wait":
                        if cnt[id(sem)] >= val:
                            pos[e] += 1
                            prog_found = True
                        else:
                            break
                    else:
                        bid = val
                        ok = all(
                            not any(k == "barrier" and v < bid
                                    for k, _, v in logs[e2][pos[e2]:])
                            for e2 in ENGINES)
                        if ok:
                            pos[e] += 1
                            prog_found = True
                        else:
                            break
            if all(pos[e] >= len(logs[e]) for e in ENGINES):
                return True
            if not prog_found:
                print("DEADLOCK detected; engine heads:")
                names = {id(s): n for n, s in self.sems.items()}
                for e in ENGINES:
                    if pos[e] < len(logs[e]):
                        kind, sem, val = logs[e][pos[e]]
                        cur = cnt[id(sem)] if sem is not None else None
                        print(f"  {e}[{pos[e]}/{len(logs[e])}]: {kind} "
                              f"{names.get(id(sem), '?') if sem is not None else ''} "
                              f"need {val} have {cur}")
                return False

    def mms(self, sem, fns):
        for f in fns[:-1]:
            self.run("tensor", f)
        return self.inc("tensor", sem, fns[-1])

    def flush(self):
        for e in ENGINES:
            self.audit[e].append(("barrier", None, self.nblocks))
        self.nblocks += 1
        ops = self.ops
        self.ops = {e: [] for e in ENGINES}
        with self.nc.Block() as blk:
            for name in ENGINES:
                lst = ops[name]
                if lst:
                    getattr(blk, name)(lambda eng, lst=lst: [f(eng) for f in lst])


def build_program(n_sub=N_SUB):
    nc = bass.Bass("TRN2", target_bir_lowering=False)
    ctx = ExitStack()
    dram, sbuf = {}, {}
    uid = [0]

    def din(name, shape, dt=F32):
        dram[name] = nc.dram_tensor(name, shape, dt, kind="ExternalInput")

    def dout(name, shape, dt=F32):
        dram[name] = nc.dram_tensor(name, shape, dt, kind="ExternalOutput")

    def dtmp(name, shape, dt):
        dram[name] = nc.dram_tensor(name, shape, dt)

    def sb(name, shape, dt):
        if name not in sbuf:
            sbuf[name] = ctx.enter_context(nc.sbuf_tensor("sb_" + name, shape, dt))
        return sbuf[name]

    def mk_local(lctx):
        uid[0] += 1
        tag = f"p{uid[0]}"

        def lsb(name, shape, dt):
            return lctx.enter_context(nc.sbuf_tensor(f"{tag}_{name}", shape, dt))

        def lps(name, shape, dt=F32):
            return lctx.enter_context(nc.psum_tensor(f"{tag}_{name}", shape, dt))
        return lsb, lps

    din("x_in", [T, D])
    din("w1T", [D, H0]); din("w1n", [H0, D])
    din("w2T", [H0, D]); din("w2n", [D, H0])
    din("b1pp", [128, 32]); din("b2row", [1, D])
    for w in ("wqT", "wkT", "wvT", "woT", "aqT", "akT", "avT", "apT"):
        din(w, [D, D])
    for w in ("wqTs", "wkTs", "wvTs", "aqTs", "akTs", "avTs"):
        din(w, [8, D, 128])
    for b in ("bqrow", "bkrow", "bvrow", "borow"):
        din(b, [1, D])
    din("wgT", [D, I1]); din("wuT", [D, I1]); din("wdT", [I1, D])
    din("wgn", [I1, D]); din("wun", [I1, D]); din("wdn", [D, I1])
    din("clips", [1, 2])
    din("identity", [128, 128], BF16)
    din("identityf", [128, 128])
    din("ropeCq", [128, T], BF16); din("ropeSq", [128, T], BF16)
    nm0 = attn_mask_bank(0)[0].shape[0]
    nm1 = attn_mask_bank(1)[0].shape[0]
    din("mask0", [nm0, 128, 512], BF16)
    din("mask1", [nm1, 128, 512], BF16)
    din("constf", [128, 8])
    din("ones1", [1, 128], BF16)
    din("ones1f", [1, 128])
    din("onesrow", [1, 512], BF16)
    din("ones2", [128, 2], BF16)
    dout("y", [T, D])
    if DEBUG_TAPS:
        dout("dbg2_xnTf", [D, T])
        dout("dbg2_h", [I1, T], BF16)
        dout("dbg2_g", [D, I1], BF16)
        dout("dbg2_u", [D, I1], BF16)
        dout("dbg2_d", [I1, D], BF16)
        dout("dbg2_xq", [128, NC * T], BF16)
        dout("dbg2_scal", [1, 16])


    dtmp("scr_w1s", [D, H0], BF16)
    dtmp("scr_w2s", [H0, D], BF16)
    dtmp("scr_g", [D, I1], BF16)
    dtmp("scr_u", [D, I1], BF16)
    dtmp("scr_d", [I1, D], BF16)
    dtmp("scr_xnT", [D, T], F32)
    dtmp("scr_h", [I1, T], BF16)
    dtmp("cc_in", [1, 16], F32)
    dtmp("cc_out", [1, 16], F32)

    sb("x", [128, NT * D], F32)
    sb("xnT", [128, NC * T], BF16)
    sb("ident", [128, 128], BF16)
    sb("identf", [128, 128], F32)
    sb("constf", [128, 8], F32)
    sb("ones1", [1, 128], BF16)
    sb("onesrow", [1, 512], BF16)
    sb("ones2", [128, 2], BF16)
    sb("st_sum", [128, NT], F32)
    sb("st_sq", [128, NT], F32)
    sb("st_a", [128, NT], F32)
    sb("st_b", [128, NT], F32)
    sb("st_t", [128, NT], F32)
    sb("st_m", [128, NT], F32)
    sb("dump", [128, D], BF16)
    sb("alpha1", [128, 32], F32)
    sb("b1pp", [128, 32], F32)
    sb("b2rowb", [1, D], BF16)
    sb("bqrowb", [1, D], BF16)
    sb("bkrowb", [1, D], BF16)
    sb("bvrowb", [1, D], BF16)
    sb("borowb", [1, D], BF16)
    sb("onescol", [128, 1], BF16)
    sb("xn0", [128, D], BF16)
    sb("xn1", [128, D], BF16)
    sb("sgu_sc", [128, 44], F32)
    sb("sgx", [128, 44], F32)
    sb("sdrow", [128, 8], F32)
    sb("scal", [1, 16], F32)

    prog = Prog(nc, ctx)
    S = prog.sems
    cst = sbuf["constf"]
    C_HALF, C_ONE, C_NEG1 = cst[:, 0:1], cst[:, 1:2], cst[:, 2:3]

    for j in range(NT):
        prog.dma("sync", S["sd"], sbuf["x"][:, j * D:(j + 1) * D],
                 dram["x_in"][j * 128:(j + 1) * 128, :])
    for nm, dst in (("identity", "ident"), ("identityf", "identf"),
                    ("constf", "constf"), ("ones1", "ones1"),
                    ("onesrow", "onesrow"), ("ones2", "ones2"), ("b1pp", "b1pp")):
        prog.dma("sync", S["sd"], sbuf[dst][:, :], dram[nm][:, :])
    prog.dma("gpsimd", S["sg"], sbuf["b2rowb"][:, :], dram["b2row"][:, :])
    for src, dst in (("bqrow", "bqrowb"), ("bkrow", "bkrowb"),
                     ("bvrow", "bvrowb"), ("borow", "borowb")):
        prog.dma("gpsimd", S["sg"], sbuf[dst][:, :], dram[src][:, :])
    prog.wait("sync", S["sd"])
    prog.wait("gpsimd", S["sg"])
    prog.run("vector", lambda e: e.memset(sbuf["onescol"][:, :], 1.0))
    prog.flush()

    def emit_ln(fp32_scratch=False):
        for j in range(NT):
            tile = sbuf["x"][:, j * D:(j + 1) * D]
            prog.wait("scalar", S["ss"])
            prog.inc("scalar", S["ss"], lambda e, t=tile, j=j: e.activation(
                out=sbuf["dump"][:, :], in_=t, func=AF.Square,
                accum_out=sbuf["st_sq"][:, j:j + 1]))
            prog.inc("vector", S["sv"], lambda e, t=tile, j=j: e.reduce_sum(
                sbuf["st_sum"][:, j:j + 1], t, axis=AX.X))
        prog.wait("vector", S["ss"])
        prog.step("vector", lambda e: e.tensor_scalar(
            out=sbuf["st_m"][:, :], in0=sbuf["st_sum"][:, :], scalar1=1.0 / D,
            scalar2=None, op0=ALU.mult))
        prog.step("vector", lambda e: e.tensor_scalar(
            out=sbuf["st_t"][:, :], in0=sbuf["st_sq"][:, :], scalar1=1.0 / D,
            scalar2=None, op0=ALU.mult))
        prog.step("vector", lambda e: e.tensor_tensor(
            out=sbuf["dump"][:, 0:NT], in0=sbuf["st_m"][:, :], in1=sbuf["st_m"][:, :],
            op=ALU.mult))
        prog.step("vector", lambda e: e.tensor_sub(
            sbuf["st_t"][:, :], sbuf["st_t"][:, :], sbuf["dump"][:, 0:NT]))
        prog.step("vector", lambda e: e.tensor_scalar(
            out=sbuf["st_t"][:, :], in0=sbuf["st_t"][:, :], scalar1=1e-5,
            scalar2=None, op0=ALU.add))
        prog.wait("scalar", S["sv"])
        prog.step("scalar", lambda e: e.activation(
            out=sbuf["dump"][:, 0:NT], in_=sbuf["st_t"][:, :], func=AF.Sqrt))
        prog.wait("vector", S["ss"])
        prog.step("vector", lambda e: e.reciprocal(
            sbuf["st_a"][:, :], sbuf["dump"][:, 0:NT]))
        prog.step("vector", lambda e: e.tensor_mul(
            sbuf["st_b"][:, :], sbuf["st_m"][:, :], sbuf["st_a"][:, :]))
        prog.step("vector", lambda e: e.tensor_scalar(
            out=sbuf["st_b"][:, :], in0=sbuf["st_b"][:, :], scalar1=-1.0,
            scalar2=None, op0=ALU.mult))
        if fp32_scratch:
            prog.run("vector", lambda e: e.memset(sbuf["st_t"][:, 0:1], 0.0))
        prog.flush()

        with ExitStack() as lctx:
            lsb, lps = mk_local(lctx)
            xn = [sbuf["xn0"], sbuf["xn1"]]
            tp = [lps("tp0", [128, D], BF16), lps("tp1", [128, D], BF16)]
            if fp32_scratch:
                xnf = lsb("xnf", [128, D], F32)
                xnfT = lsb("xnfT", [128, D], F32)
                tpf = [lps("tpf0", [128, D], F32), lps("tpf1", [128, D], F32)]
            xnT = sbuf["xnT"]
            xn_free = [0, 0]
            tp_free = [0, 0]
            tpf_free = [0, 0]
            xnf_free = 0
            xnfT_free = 0
            for j in range(NT):
                b = j % 2
                tile = sbuf["x"][:, j * D:(j + 1) * D]
                if xn_free[b]:
                    prog.wait("vector", S["sp"], xn_free[b])
                if fp32_scratch:
                    if xnf_free:
                        prog.wait("vector", S["sp"], xnf_free)
                    prog.inc("vector", S["sv"], lambda e, t=tile, j=j: e.tensor_scalar(
                        out=xnf[:, :], in0=t, scalar1=sbuf["st_a"][:, j:j + 1],
                        scalar2=sbuf["st_b"][:, j:j + 1], op0=ALU.mult, op1=ALU.add))
                    prog.wait("vector", S["sv"])
                    prog.inc("vector", S["sv"], lambda e, b=b: e.tensor_copy(
                        out=xn[b][:, :], in_=xnf[:, :]))
                    prog.inc("vector", S["sv"], lambda e: e.reduce_max(
                        sbuf["st_m"][:, 0:1], xnf[:, :], axis=AX.X,
                        apply_absolute_value=True))
                    prog.wait("vector", S["sv"])
                    prog.inc("vector", S["sv"], lambda e: e.tensor_max(
                        sbuf["st_t"][:, 0:1], sbuf["st_t"][:, 0:1], sbuf["st_m"][:, 0:1]))
                    prog.wait("vector", S["sv"])
                else:
                    prog.inc("vector", S["sv"], lambda e, t=tile, j=j, b=b: e.tensor_scalar(
                        out=xn[b][:, :], in0=t, scalar1=sbuf["st_a"][:, j:j + 1],
                        scalar2=sbuf["st_b"][:, j:j + 1], op0=ALU.mult, op1=ALU.add))
                v_ready = S["sv"].n
                prog.wait("tensor", S["sv"], v_ready)
                if tp_free[b]:
                    prog.wait("tensor", S["sv"], tp_free[b])
                prog.mms(S["sp"], [
                    (lambda e, b=b, c=c: e.transpose(
                        tp[b][:, c * 128:(c + 1) * 128], xn[b][:, c * 128:(c + 1) * 128],
                        sbuf["ident"][:, :])) for c in range(NC)])
                xn_free[b] = S["sp"].n
                prog.wait("vector", S["sp"], S["sp"].n)
                prog.inc("vector", S["sv"], lambda e, b=b, j=j: e.tensor_copy(
                    out=bass.AP(xnT, j * 128, [[NC * T, 128], [T, NC], [1, 128]]),
                    in_=bass.AP(tp[b], 0, [[D, 128], [128, NC], [1, 128]])))
                tp_free[b] = S["sv"].n
                if fp32_scratch:
                    fb = j % 2
                    if tpf_free[fb]:
                        prog.wait("tensor", S["sv"], tpf_free[fb])
                    prog.mms(S["sp"], [
                        (lambda e, fb=fb, c=c: e.transpose(
                            tpf[fb][:, c * 128:(c + 1) * 128], xnf[:, c * 128:(c + 1) * 128],
                            sbuf["identf"][:, :])) for c in range(NC)])
                    xnf_free = S["sp"].n
                    if xnfT_free:
                        prog.wait("vector", S["sxf"], xnfT_free)
                    prog.wait("vector", S["sp"], S["sp"].n)
                    prog.inc("vector", S["sv"], lambda e, fb=fb: e.tensor_copy(
                        out=xnfT[:, :], in_=tpf[fb][:, :]))
                    tpf_free[fb] = S["sv"].n
                    prog.wait("sync", S["sv"], S["sv"].n)
                    xnfT_free = prog.dma(
                        "sync", S["sxf"],
                        bass.AP(dram["scr_xnT"], j * 128, [[T, 128], [T * 128, NC], [1, 128]]),
                        bass.AP(xnfT, 0, [[D, 128], [128, NC], [1, 128]]))
            prog.flush()

    def emit_ffn0_prep():
        with ExitStack() as lctx:
            lsb, lps = mk_local(lctx)
            wtmp = lsb("wtmp", [128, H0], F32)
            wsg = lsb("wsg", [128, D], F32)
            wsgb = lsb("wsgb", [128, H0], BF16)
            a2pp = lsb("a2pp", [128, 8], F32)
            a2row = lsb("a2row", [1, D], F32)
            a2rowb = lsb("a2rowb", [1, D], BF16)
            pb = lps("pb", [128, D], F32)
            for ic in range(NC):
                prog.dma("sync", S["sd"], wtmp[:, :],
                         dram["w1T"][ic * 128:(ic + 1) * 128, :])
                prog.wait("scalar", S["sd"])
                prog.inc("scalar", S["ss"], lambda e: e.sign(out=wsgb[:, :], in_=wtmp[:, :]))
                prog.wait("sync", S["ss"])
                prog.dma("sync", S["sd"], dram["scr_w1s"][ic * 128:(ic + 1) * 128, :],
                         wsgb[:, :])
                prog.wait("sync", S["sd"])
            for oc in range(32):
                prog.dma("sync", S["sd"], wtmp[:, 0:D],
                         dram["w1n"][oc * 128:(oc + 1) * 128, :])
                prog.wait("vector", S["sd"])
                prog.inc("vector", S["sv"], lambda e, oc=oc: e.reduce_sum(
                    sbuf["alpha1"][:, oc:oc + 1], wtmp[:, 0:D], axis=AX.X,
                    apply_absolute_value=True))
                prog.wait("sync", S["sv"])
            prog.step("vector", lambda e: e.tensor_scalar(
                out=sbuf["alpha1"][:, :], in0=sbuf["alpha1"][:, :], scalar1=1.0 / D,
                scalar2=None, op0=ALU.mult))
            for dc in range(NC):
                prog.dma("sync", S["sd"], wtmp[:, :],
                         dram["w2n"][dc * 128:(dc + 1) * 128, :])
                prog.wait("vector", S["sd"])
                prog.inc("vector", S["sv"], lambda e, dc=dc: e.reduce_sum(
                    a2pp[:, dc:dc + 1], wtmp[:, :], axis=AX.X, apply_absolute_value=True))
                prog.wait("sync", S["sv"])
            prog.step("vector", lambda e: e.tensor_scalar(
                out=a2pp[:, :], in0=a2pp[:, :], scalar1=1.0 / H0, scalar2=None,
                op0=ALU.mult))
            prog.wait("sync", S["sv"])
            for dc in range(NC):
                prog.dma("sync", S["sd"], a2row[0:1, dc * 128:(dc + 1) * 128],
                         a2pp[:, dc:dc + 1])
            prog.wait("sync", S["sd"])
            prog.wait("vector", S["sd"])
            prog.inc("vector", S["sv"], lambda e: e.tensor_copy(
                out=a2rowb[:, :], in_=a2row[:, :]))
            prog.wait("tensor", S["sv"])
            prog.mms(S["sp"], [
                lambda e: e.matmul(pb[:, 0:512], sbuf["ones1"][:, :],
                                   a2rowb[:, 0:512], start=True, stop=True),
                lambda e: e.matmul(pb[:, 512:D], sbuf["ones1"][:, :],
                                   a2rowb[:, 512:D], start=True, stop=True)])
            prog.wait("vector", S["sp"])
            for hc in range(32):
                prog.dma("sync", S["sd"], wtmp[:, 0:D],
                         dram["w2T"][hc * 128:(hc + 1) * 128, :])
                prog.wait("scalar", S["sd"])
                prog.inc("scalar", S["ss"], lambda e: e.sign(out=wsg[:, :], in_=wtmp[:, 0:D]))
                prog.wait("vector", S["ss"])
                prog.inc("vector", S["sv"], lambda e: e.tensor_tensor(
                    out=wsgb[:, 0:D], in0=wsg[:, :], in1=pb[:, :], op=ALU.mult))
                prog.wait("sync", S["sv"])
                prog.dma("sync", S["sd"], dram["scr_w2s"][hc * 128:(hc + 1) * 128, :],
                         wsgb[:, 0:D])
                prog.wait("sync", S["sd"])
            prog.flush()

    def emit_ffn0(first):
        if first:
            emit_ffn0_prep()
        emit_ln()
        with ExitStack() as lctx:
            lsb, lps = mk_local(lctx)
            w1b = [lsb(f"w1b{k}", [128, NC * 1024], BF16) for k in range(2)]
            w2b = [lsb(f"w2b{k}", [128, NC * 1024], BF16) for k in range(2)]
            hsb = [lsb(f"hsb{k}", [128, NC * 256], BF16) for k in range(2)]
            ph = [lps(f"ph{k}", [128, 256], F32) for k in range(2)]
            po = [lps(f"po{k}", [128, D], F32) for k in range(2)]
            xnT = sbuf["xnT"]
            ldone = [0, 0]

            def load_quarter(hq, buf):
                sem = S["sw%d" % buf]
                for ic in range(NC):
                    prog.dma("sync", sem, w1b[buf][:, ic * 1024:(ic + 1) * 1024],
                             dram["scr_w1s"][ic * 128:(ic + 1) * 128,
                                             hq * 1024:(hq + 1) * 1024])
                for hc in range(NC):
                    prog.dma("sync", sem, w2b[buf][:, hc * 1024:(hc + 1) * 1024],
                             dram["scr_w2s"][hq * 1024 + hc * 128:
                                             hq * 1024 + (hc + 1) * 128, :])
                ldone[buf] = sem.n

            load_quarter(0, 0)
            hs_free = [0, 0]
            ph_free = [0, 0]
            po_free = [0, 0]
            for hq in range(4):
                buf = hq % 2
                if hq < 3:
                    prog.wait("sync", S["sp"], S["sp"].n)
                    load_quarter(hq + 1, 1 - buf)
                prog.wait("tensor", S["sw%d" % buf], ldone[buf])
                for jj in range(NT // 2):
                    hset = jj % 2
                    if hs_free[hset]:
                        prog.wait("scalar", S["sp"], hs_free[hset])
                    for hc in range(NC):
                        bk = hc % 2
                        if ph_free[bk]:
                            prog.wait("tensor", S["ss"], ph_free[bk])
                        prog.mms(S["sp"], [
                            (lambda e, buf=buf, hc=hc, ic=ic, jj=jj, bk=bk: e.matmul(
                                ph[bk][:, :],
                                w1b[buf][:, ic * 1024 + hc * 128:ic * 1024 + (hc + 1) * 128],
                                xnT[:, ic * T + jj * 256:ic * T + (jj + 1) * 256],
                                start=(ic == 0), stop=(ic == NC - 1)))
                            for ic in range(NC)])
                        prog.wait("scalar", S["sp"], S["sp"].n)
                        prog.inc("scalar", S["ss"], lambda e, bk=bk, hset=hset, hc=hc, hq=hq:
                                 e.activation(out=hsb[hset][:, hc * 256:(hc + 1) * 256],
                                              in_=ph[bk][:, :], func=AF.Gelu,
                                              bias=sbuf["b1pp"][:, hq * 8 + hc:hq * 8 + hc + 1],
                                              scale=sbuf["alpha1"][:, hq * 8 + hc:hq * 8 + hc + 1]))
                        ph_free[bk] = S["ss"].n
                    prog.wait("tensor", S["ss"], S["ss"].n)
                    for ts in range(2):
                        j = jj * 2 + ts
                        pbk = ts
                        if po_free[pbk]:
                            prog.wait("tensor", S["sv"], po_free[pbk])
                        last_is_bias = (hq == 3)
                        fns = []
                        for hc in range(NC):
                            for dc2 in range(2):
                                stop = (hc == NC - 1 and not last_is_bias)
                                fns.append(lambda e, hset=hset, hc=hc, ts=ts, dc2=dc2,
                                           buf=buf, pbk=pbk, stop=stop: e.matmul(
                                               po[pbk][:, dc2 * 512:(dc2 + 1) * 512],
                                               hsb[hset][:, hc * 256 + ts * 128:
                                                         hc * 256 + (ts + 1) * 128],
                                               w2b[buf][:, hc * 1024 + dc2 * 512:
                                                        hc * 1024 + (dc2 + 1) * 512],
                                               start=(hc == 0), stop=stop))
                        if last_is_bias:
                            for dc2 in range(2):
                                fns.append(lambda e, pbk=pbk, dc2=dc2: e.matmul(
                                    po[pbk][:, dc2 * 512:(dc2 + 1) * 512],
                                    sbuf["ones1"][:, :],
                                    sbuf["b2rowb"][:, dc2 * 512:(dc2 + 1) * 512],
                                    start=False, stop=True))
                        prog.mms(S["sp"], fns)
                        prog.wait("vector", S["sp"], S["sp"].n)
                        prog.inc("vector", S["sv"], lambda e, j=j, pbk=pbk:
                                 e.scalar_tensor_tensor(
                                     out=sbuf["x"][:, j * D:(j + 1) * D], in0=po[pbk][:, :],
                                     scalar=C_HALF, in1=sbuf["x"][:, j * D:(j + 1) * D],
                                     op0=ALU.mult, op1=ALU.add))
                        po_free[pbk] = S["sv"].n
                    hs_free[hset] = S["sp"].n
            prog.flush()

    def emit_attn(layer):
        emit_ln()
        tile_sets = attn_tile_sets(layer)
        _, tile_mask = attn_mask_bank(layer)
        maskbank = dram["mask0" if layer == 0 else "mask1"]
        wq_s = dram["wqTs" if layer == 0 else "aqTs"]
        wk_s = dram["wkTs" if layer == 0 else "akTs"]
        wv_s = dram["wvTs" if layer == 0 else "avTs"]
        wo_in = dram["woT" if layer == 0 else "apT"]
        has_bias = (layer == 0)
        has_rope = (layer == 0)
        xnT = sbuf["xnT"]
        with ExitStack() as actx:
            asb, aps = mk_local(actx)
            nmask = attn_mask_bank(layer)[0].shape[0]
            masks = asb("masks", [128, nmask * 512], BF16)
            for mi in range(nmask):
                prog.dma("sync", S["sd"], masks[:, mi * 512:(mi + 1) * 512],
                         maskbank[mi, :, :])
            attnout = asb("attnout", [128, NC * T], BF16)
            hctx = ExitStack()
            ropeC = ropeS = None
            hsb, _ = mk_local(hctx)
            wsl = [[hsb(f"wsl{w}{k}", [128, NC * 128], BF16) for k in range(2)]
                   for w in range(3)]
            qraw = hsb("qraw", [128, T], BF16)
            kraw = hsb("kraw", [128, T], BF16)
            qsw = hsb("qsw", [128, T], BF16)
            qsq = hsb("qsq", [128, 512], BF16)
            qbuf = hsb("qbuf", [65, T], BF16)
            kbuf = hsb("kbuf", [65, T], BF16)
            v65 = hsb("v65", [128, NT * 130], BF16)
            nrm = hsb("nrm", [2, T], BF16)
            nscal = hsb("nscal", [2, 8], F32)
            if has_rope:
                ropeC = hsb("ropeC", [128, T], BF16)
                ropeS = hsb("ropeS", [128, T], BF16)
                prog.dma("sync", S["sd"], ropeC[:, :], dram["ropeCq"][:, :])
                prog.dma("sync", S["sd"], ropeS[:, :], dram["ropeSq"][:, :])
                prog.wait("sync", S["sd"])
                prog.wait("vector", S["sd"])
            pt = [hsb(f"pt{k}", [128, 1024], BF16) for k in range(2)]
            avsb = hsb("avsb", [64, 512], BF16)
            rrow = hsb("rrow", [1, 512], BF16)
            pq = [aps(f"pq{k}", [128, 512], F32) for k in range(2)]
            pn = aps("pn", [2, 512], F32)
            avps = aps("avps", [128, 512], F32)
            ps3 = [aps(f"ps3{k}", [128, 1024], F32) for k in range(2)]
            crow2 = qsw
            avsb2 = hsb("avsb2", [64, 512], BF16)
            ldone = [0, 0]

            def load_slices(hp, buf):
                sem = S["sg%d" % buf]
                for w, src in ((0, wq_s), (1, wk_s), (2, wv_s)):
                    prog.dma("gpsimd", sem, bass.AP(
                        wsl[w][buf], 0, [[NC * 128, 128], [128, NC], [1, 128]]),
                        bass.AP(src, hp * D * 128, [[128, 128], [128 * 128, NC], [1, 128]]))
                ldone[buf] = sem.n

            mask_ld = S["sd"].n
            prog.wait("sync", S["sd"], mask_ld)
            prog.wait("vector", S["sd"], mask_ld)
            avsb2_dma = [0]
            load_slices(0, 0)
            for hp in range(NC):
                buf = hp % 2
                if hp < NC - 1:
                    prog.wait("gpsimd", S["sp"], S["sp"].n)
                    load_slices(hp + 1, 1 - buf)
                prog.wait("tensor", S["sg%d" % buf], ldone[buf])
                prog.wait("tensor", S["sv"])
                prog.wait("tensor", S["ss"])
                prog.run("vector", lambda e: e.memset(nscal[0:2, 2:3], 0.0))
                pq_user_v = [0, 0]
                pn_user = 0
                pn_user_s = 0
                qsq_user = 0
                for wi, raw, brow in ((0, qraw, "bqrowb"),
                                      (1, kraw, "bkrowb")):
                    for tt in range(4):
                        bk = tt % 2
                        if pq_user_v[bk]:
                            prog.wait("tensor", S["sv"], pq_user_v[bk])
                        fns = []
                        for ic in range(NC):
                            last = (ic == NC - 1 and not has_bias)
                            fns.append(lambda e, wi=wi, buf=buf, ic=ic, tt=tt, bk=bk,
                                       last=last: e.matmul(
                                           pq[bk][:, :],
                                           wsl[wi][buf][:, ic * 128:(ic + 1) * 128],
                                           xnT[:, ic * T + tt * 512:ic * T + (tt + 1) * 512],
                                           start=(ic == 0), stop=last))
                        if has_bias:
                            fns.append(lambda e, bk=bk, brow=brow, hp=hp: e.matmul(
                                pq[bk][:, :],
                                sbuf[brow][0:1, hp * 128:(hp + 1) * 128],
                                sbuf["onesrow"][:, :], start=False, stop=True))
                        prog.mms(S["sp"], fns)
                        mmv = S["sp"].n
                        prog.wait("vector", S["sp"], mmv)
                        if wi == 0 and not has_rope:
                            prog.inc("vector", S["sv"], lambda e, bk=bk, raw=raw, tt=tt:
                                     e.tensor_scalar(out=raw[:, tt * 512:(tt + 1) * 512],
                                                     in0=pq[bk][:, :], scalar1=0.125,
                                                     scalar2=None, op0=ALU.mult))
                        else:
                            prog.inc("vector", S["sv"], lambda e, bk=bk, raw=raw, tt=tt:
                                     e.tensor_copy(out=raw[:, tt * 512:(tt + 1) * 512],
                                                   in_=pq[bk][:, :]))
                        pq_user_v[bk] = S["sv"].n
                        if "norm" not in ATTN_PARTS:
                            continue
                        if qsq_user:
                            prog.wait("vector", S["sp"], qsq_user)
                        prog.wait("vector", S["sv"])
                        prog.inc("vector", S["sv"], lambda e, raw=raw, tt=tt:
                                 e.tensor_mul(qsq[:, :],
                                              raw[:, tt * 512:(tt + 1) * 512],
                                              raw[:, tt * 512:(tt + 1) * 512]))
                        sqv = S["sv"].n
                        prog.wait("tensor", S["sv"], sqv)
                        if pn_user:
                            prog.wait("tensor", S["sv"], pn_user)
                        if pn_user_s:
                            prog.wait("tensor", S["ss"], pn_user_s)
                        prog.inc("tensor", S["sp"], lambda e: e.matmul(
                            pn[:, :], sbuf["ones2"][:, :], qsq[:, :],
                            start=True, stop=True))
                        qsq_user = S["sp"].n
                        if wi == 0:
                            prog.wait("scalar", S["sp"], S["sp"].n)
                            prog.inc("scalar", S["ss"], lambda e, tt=tt:
                                     e.activation(out=nrm[:, tt * 512:(tt + 1) * 512],
                                                  in_=pn[:, :], func=AF.Sqrt))
                            pn_user_s = S["ss"].n
                        else:
                            prog.wait("vector", S["sp"], S["sp"].n)
                            prog.step("vector", lambda e: e.reduce_max(
                                nscal[0:2, 3:4], pn[:, :], axis=AX.X))
                            prog.step("vector", lambda e: e.tensor_max(
                                nscal[0:2, 2:3], nscal[0:2, 2:3], nscal[0:2, 3:4]))
                            pn_user = S["sv"].n
                if "v" not in ATTN_PARTS:
                    continue
                prog.wait("tensor", S["ss"], S["ss"].n)
                prog.wait("tensor", S["sv"], S["sv"].n)
                v_evac = [0, 0]
                for jt in range(NT):
                    bk = jt % 2
                    dst = pq[bk][:, 0:128]
                    if v_evac[bk]:
                        prog.wait("tensor", S["sv"], v_evac[bk])
                    fns = []
                    for ic in range(NC):
                        last = (ic == NC - 1 and not has_bias)
                        fns.append(lambda e, buf=buf, ic=ic, jt=jt, dst=dst, last=last:
                                   e.matmul(dst,
                                            xnT[:, ic * T + jt * 128:ic * T + (jt + 1) * 128],
                                            wsl[2][buf][:, ic * 128:(ic + 1) * 128],
                                            start=(ic == 0), stop=last))
                    if has_bias:
                        fns.append(lambda e, dst=dst, hp=hp: e.matmul(
                            dst, sbuf["ones1"][:, :],
                            sbuf["bvrowb"][0:1, hp * 128:(hp + 1) * 128],
                            start=False, stop=True))
                    prog.mms(S["sp"], fns)
                    prog.wait("vector", S["sp"], S["sp"].n)
                    prog.inc("vector", S["sv"], lambda e, bk=bk, jt=jt: e.tensor_copy(
                        out=bass.AP(v65, jt * 130, [[NT * 130, 128], [65, 2], [1, 64]]),
                        in_=bass.AP(pq[bk], 0, [[512, 128], [64, 2], [1, 64]])))
                    v_evac[bk] = S["sv"].n
                prog.run("vector", lambda e: e.memset(
                    bass.AP(v65, 64, [[NT * 130, 128], [65, 2 * NT], [1, 1]]), 1.0))
                prog.inc("vector", S["sv"], lambda e: e.engine_nop())
                if ATTN_STOP <= 1:
                    continue
                if has_rope:
                    for raw in (qraw, kraw):
                        prog.wait("sync", S["sv"], S["sv"].n)
                        for h0 in (0, 64):
                            prog.dma("sync", S["sd"], qsw[h0:h0 + 32, :],
                                     raw[h0 + 32:h0 + 64, :])
                            prog.dma("sync", S["sd"], qsw[h0 + 32:h0 + 64, :],
                                     raw[h0:h0 + 32, :])
                        prog.wait("vector", S["sd"])
                        prog.step("vector", lambda e: e.tensor_tensor(
                            out=qsw[:, :], in0=qsw[:, :], in1=ropeS[:, :],
                            op=ALU.mult))
                        prog.step("vector", lambda e, raw=raw: e.tensor_tensor(
                            out=raw[:, :], in0=raw[:, :], in1=ropeC[:, :],
                            op=ALU.mult))
                        prog.step("vector", lambda e, raw=raw: e.tensor_add(
                            raw[:, :], raw[:, :], qsw[:, :]))
                    prog.step("vector", lambda e: e.tensor_scalar(
                        out=kraw[:, :], in0=kraw[:, :], scalar1=8.0, scalar2=None,
                        op0=ALU.mult))
                if ATTN_STOP <= 2 or (ATTN_STOP <= 3 and hp > 0):
                    continue
                prog.wait("scalar", S["sv"])
                prog.step("scalar", lambda e: e.activation(
                    out=nscal[0:2, 0:1], in_=nscal[0:2, 2:3], func=AF.Sqrt))
                prog.wait("vector", S["ss"])
                prog.step("vector", lambda e: e.tensor_scalar(
                    out=nscal[0:2, 1:2], in0=nscal[0:2, 0:1], scalar1=-0.125,
                    scalar2=None, op0=ALU.mult))
                prog.step("vector", lambda e: e.tensor_scalar(
                    out=crow2[0:2, 0:T], in0=nrm[0:2, 0:T],
                    scalar1=nscal[0:2, 1:2], scalar2=None, op0=ALU.mult))
                prog.run("vector", lambda e: e.memset(kbuf[64:65, :], 1.0))
                prog.inc("vector", S["sv"], lambda e: e.engine_nop())
                crow_v = S["sv"].n
                ps3_exp = [0, 0]
                pt_av = [0, 0]
                pt_msk = [0, 0]
                av_user = 0
                av_user_s = 0
                qk_dma = 0
                for sub in range(2):
                    head = 2 * hp + sub
                    prog.wait("sync", S["sv"], crow_v)
                    prog.wait("sync", S["sp"], S["sp"].n)
                    prog.dma("sync", S["sd"], kbuf[0:64, :],
                             kraw[sub * 64:(sub + 1) * 64, :])
                    prog.dma("sync", S["sd"], qbuf[0:64, :],
                             qraw[sub * 64:(sub + 1) * 64, :])
                    prog.dma("sync", S["sd"], qbuf[64:65, :], crow2[sub:sub + 1, :])
                    qk_dma = S["sd"].n
                    prog.wait("tensor", S["sd"], qk_dma)
                    for qt in range(4):
                        tiles = tile_sets[qt]
                        batches = [tiles[i:i + 2] for i in range(0, len(tiles), 2)]
                        first_av = True
                        av_emitted = 0
                        pending = None

                        def emit_av(p):
                            nonlocal first_av, av_emitted
                            pbatch, par, ev, msk = p
                            if msk:
                                prog.wait("tensor", S["sv"], msk)
                            else:
                                prog.wait("tensor", S["ss"], ev)
                            if first_av and (av_user or av_user_s):
                                prog.wait("tensor", S["sv"], av_user)
                                prog.wait("tensor", S["ss"], av_user_s)
                            fns = []
                            for ti, (kt, _) in enumerate(pbatch):
                                st = first_av and ti == 0
                                av_emitted += 1
                                fin = (av_emitted == len(tiles))
                                fns.append(lambda e, par=par, ti=ti, kt=kt, st=st,
                                           sub=sub, fin=fin: e.matmul(
                                               avps[0:65, :],
                                               bass.AP(v65, kt * 130 + sub * 65,
                                                       [[NT * 130, 128], [1, 65]]),
                                               pt[par][:, ti * 512:(ti + 1) * 512],
                                               start=st, stop=fin))
                            first_av = False
                            prog.mms(S["sp"], fns)
                            pt_av[par] = S["sp"].n

                        for bi, batch in enumerate(batches):
                            par = bi % 2
                            if ps3_exp[par]:
                                prog.wait("tensor", S["ss"], ps3_exp[par])
                            fns = []
                            for ti, (kt, _) in enumerate(batch):
                                fns.append(lambda e, par=par, ti=ti, kt=kt, qt=qt:
                                           e.matmul(ps3[par][:, ti * 512:(ti + 1) * 512],
                                                    kbuf[:, kt * 128:(kt + 1) * 128],
                                                    qbuf[:, qt * 512:(qt + 1) * 512],
                                                    start=True, stop=True))
                            prog.mms(S["sp"], fns)
                            sc_done = S["sp"].n
                            prog.wait("scalar", S["sp"], sc_done)
                            if pt_av[par]:
                                prog.wait("scalar", S["sp"], pt_av[par])
                                if pt_msk[par]:
                                    prog.wait("scalar", S["sv"], pt_msk[par])
                            nb = len(batch)
                            prog.inc("scalar", S["ss"], lambda e, par=par, nb=nb:
                                     e.activation(out=pt[par][:, 0:nb * 512],
                                                  in_=ps3[par][:, 0:nb * 512],
                                                  func=AF.Exp))
                            ev = S["ss"].n
                            ps3_exp[par] = ev
                            need_mask = [(ti, tile_mask[(qt, kt)])
                                         for ti, (kt, _) in enumerate(batch)
                                         if tile_mask[(qt, kt)] is not None]
                            msk = 0
                            if need_mask:
                                prog.wait("vector", S["ss"], ev)
                                for ti, mi in need_mask:
                                    prog.inc("vector", S["sv"],
                                             lambda e, par=par, ti=ti, mi=mi:
                                             e.tensor_tensor(
                                                 out=pt[par][:, ti * 512:(ti + 1) * 512],
                                                 in0=pt[par][:, ti * 512:(ti + 1) * 512],
                                                 in1=masks[:, mi * 512:(mi + 1) * 512],
                                                 op=ALU.mult))
                                pt_msk[par] = S["sv"].n
                                msk = pt_msk[par]
                            if pending is not None:
                                emit_av(pending)
                            pending = (batch, par, ev, msk)
                        emit_av(pending)
                        av_done = S["sp"].n
                        prog.wait("vector", S["sp"], av_done)

                        def _recip(e):
                            with nc.allow_low_precision("softmax 1/l in bf16"):
                                return e.reciprocal(rrow[:, :], avps[64:65, :])
                        prog.step("vector", _recip)
                        rv = S["sv"].n
                        prog.wait("tensor", S["sv"], rv)
                        prog.inc("tensor", S["sp"], lambda e: e.matmul(
                            pq[0][0:64, :], sbuf["ones1"][0:1, 0:64], rrow[:, :],
                            start=True, stop=True))
                        rbcv = S["sp"].n
                        prog.wait("scalar", S["sp"], av_done)
                        prog.wait("scalar", S["sv"], rv)
                        prog.inc("scalar", S["ss"], lambda e: e.copy(
                            avsb[:, :], avps[0:64, :]))
                        av_user_s = S["ss"].n
                        prog.wait("vector", S["ss"], av_user_s)
                        prog.wait("vector", S["sp"], rbcv)
                        if sub == 0:
                            prog.inc("vector", S["sv"], lambda e, hp=hp, qt=qt:
                                     e.tensor_tensor(
                                         out=attnout[0:64,
                                                     hp * T + qt * 512:hp * T + (qt + 1) * 512],
                                         in0=avsb[:, :], in1=pq[0][0:64, :], op=ALU.mult))
                        else:
                            prog.wait("vector", S["sab"], avsb2_dma[0])
                            prog.inc("vector", S["sv"], lambda e: e.tensor_tensor(
                                out=avsb2[:, :], in0=avsb[:, :], in1=pq[0][0:64, :],
                                op=ALU.mult))
                            prog.wait("sync", S["sv"], S["sv"].n)
                            avsb2_dma[0] = prog.dma(
                                "sync", S["sab"],
                                attnout[64:128, hp * T + qt * 512:hp * T + (qt + 1) * 512],
                                avsb2[:, :])
                        av_user = S["sv"].n
            if ATTN_STOP <= 4:
                hctx.close()
                prog.flush()
                return
            hctx.close()
            wob = asb("wob", [128, NC * 1024], BF16)
            prog.wait("gpsimd", S["sp"], S["sp"].n)
            prog.wait("gpsimd", S["ss"])
            prog.wait("gpsimd", S["sv"])
            for ic in range(NC):
                prog.dma("gpsimd", S["sg"], wob[:, ic * 1024:(ic + 1) * 1024],
                         wo_in[ic * 128:(ic + 1) * 128, :])
            prog.wait("tensor", S["sg"])
            prog.wait("tensor", S["sv"])
            prog.wait("tensor", S["ss"])
            prog.wait("tensor", S["sd"])
            prog.wait("tensor", S["sab"])
            po = ps3
            po_free = [0, 0]
            for j in range(NT):
                pbk = j % 2
                if po_free[pbk]:
                    prog.wait("tensor", S["sv"], po_free[pbk])
                fns = []
                for ic in range(NC):
                    last = (ic == NC - 1 and not has_bias)
                    for dc2 in range(2):
                        fns.append(lambda e, ic=ic, j=j, dc2=dc2, pbk=pbk,
                                   last=last: e.matmul(
                                       po[pbk][:, dc2 * 512:(dc2 + 1) * 512],
                                       attnout[:, ic * T + j * 128:ic * T + (j + 1) * 128],
                                       wob[:, ic * 1024 + dc2 * 512:ic * 1024 + (dc2 + 1) * 512],
                                       start=(ic == 0), stop=last))
                if has_bias:
                    for dc2 in range(2):
                        fns.append(lambda e, pbk=pbk, dc2=dc2: e.matmul(
                            po[pbk][:, dc2 * 512:(dc2 + 1) * 512],
                            sbuf["ones1"][:, :],
                            sbuf["borowb"][:, dc2 * 512:(dc2 + 1) * 512],
                            start=False, stop=True))
                prog.mms(S["sp"], fns)
                prog.wait("vector", S["sp"], S["sp"].n)
                prog.inc("vector", S["sv"], lambda e, j=j, pbk=pbk: e.tensor_add(
                    sbuf["x"][:, j * D:(j + 1) * D], po[pbk][:, :],
                    sbuf["x"][:, j * D:(j + 1) * D]))
                po_free[pbk] = S["sv"].n
            prog.flush()

    PIECES = [(0, 6), (6, 6), (12, 5), (17, 5)]

    def _ffn1_commons():
        new = "ones1f" not in sbuf
        sb("sc_misc", [128, 8], F32)
        sb("grow", [1, 128], F32)
        sb("ccs", [1, 16], F32)
        sb("ones1f", [1, 128], F32)
        if new:
            prog.dma("sync", S["sd"], sbuf["ones1f"][:, :], dram["ones1f"][:, :])
            prog.wait("sync", S["sd"])
            prog.wait("tensor", S["sd"])

    def bcast_pp(dst_sb, src_11, psum_t):
        prog.wait("tensor", S["sv"])
        prog.inc("tensor", S["sp"], lambda e: e.matmul(
            psum_t[:, 0:1], sbuf["ones1f"][:, :], src_11, start=True, stop=True))
        prog.wait("vector", S["sp"])
        prog.step("vector", lambda e: e.tensor_copy(out=dst_sb, in_=psum_t[:, 0:1]))

    def emit_allreduce_max(src_pp, dst_11):
        prog.wait("sync", S["sv"])
        prog.dma("sync", S["sd"], sbuf["grow"][0:1, 0:128], src_pp)
        prog.wait("vector", S["sd"])
        prog.step("vector", lambda e: e.reduce_max(
            sbuf["scal"][0:1, 0:1], sbuf["grow"][0:1, 0:128], axis=AX.X))
        prog.step("vector", lambda e: e.memset(sbuf["ccs"][:, :], 0.0))
        prog.step("vector", lambda e: e.tensor_copy(
            out=sbuf["ccs"][0:1, 0:1], in_=sbuf["scal"][0:1, 0:1]))
        prog.wait("sync", S["sv"])
        prog.dma("sync", S["sd"], dram["cc_in"][:, :], sbuf["ccs"][:, :])
        prog.wait("gpsimd", S["sd"])
        prog.inc("gpsimd", S["scc"], lambda e: e.collective_compute(
            "AllReduce", ALU.max, replica_groups=[list(range(8))],
            ins=[dram["cc_in"].ap()], outs=[dram["cc_out"].ap()]))
        prog.wait("sync", S["scc"])
        prog.dma("sync", S["sd"], sbuf["ccs"][:, :], dram["cc_out"][:, :])
        prog.wait("vector", S["sd"])
        prog.step("vector", lambda e: e.tensor_copy(out=dst_11, in_=sbuf["ccs"][0:1, 0:1]))

    def emit_ffn1_prep():
        with ExitStack() as lctx:
            lsb, lps = mk_local(lctx)
            wtmp = lsb("wtmp", [128, I1], F32)
            wint = lsb("wint", [128, 1024], I32)
            wbf = lsb("wbf", [128, 1024], BF16)
            wtr = lsb("wtr", [128, 1024], BF16)
            ptr = lps("ptr", [128, 1024], BF16)
            for wn, col0 in (("wgn", 0), ("wun", 22)):
                for oc in range(NI1):
                    prog.dma("sync", S["sd"], wtmp[:, 0:D],
                             dram[wn][oc * 128:(oc + 1) * 128, :])
                    prog.wait("vector", S["sd"])
                    prog.inc("vector", S["sv"], lambda e, oc=oc, col0=col0: e.reduce_max(
                        sbuf["sgu_sc"][:, col0 + oc:col0 + oc + 1], wtmp[:, 0:D],
                        axis=AX.X, apply_absolute_value=True))
                    prog.wait("sync", S["sv"])
            for dc in range(NC):
                prog.dma("sync", S["sd"], wtmp[:, :],
                         dram["wdn"][dc * 128:(dc + 1) * 128, :])
                prog.wait("vector", S["sd"])
                prog.inc("vector", S["sv"], lambda e, dc=dc: e.reduce_max(
                    sbuf["sdrow"][:, dc:dc + 1], wtmp[:, :], axis=AX.X,
                    apply_absolute_value=True))
                prog.wait("sync", S["sv"])
            prog.wait("vector", S["sv"])
            prog.step("vector", lambda e: e.tensor_scalar(
                out=sbuf["sgu_sc"][:, :], in0=sbuf["sgu_sc"][:, :], scalar1=1.0 / 127.0,
                scalar2=None, op0=ALU.mult))
            prog.step("vector", lambda e: e.tensor_scalar(
                out=sbuf["sdrow"][:, :], in0=sbuf["sdrow"][:, :], scalar1=1.0 / 127.0,
                scalar2=None, op0=ALU.mult))
            prog.step("vector", lambda e: e.reciprocal(
                sbuf["sgx"][:, :], sbuf["sgu_sc"][:, :]))
            prog.step("vector", lambda e: e.reciprocal(
                sbuf["sc_misc"][:, 0:8], sbuf["sdrow"][:, :]))
            prog.flush()
            for wn, scr, col0 in (("wgn", "scr_g", 0), ("wun", "scr_u", 22)):
                for oc in range(NI1):
                    prog.wait("sync", S["sp"], S["sp"].n)
                    prog.dma("sync", S["sd"], wtmp[:, 0:D],
                             dram[wn][oc * 128:(oc + 1) * 128, :])
                    prog.wait("vector", S["sd"])
                    prog.step("vector", lambda e, oc=oc, col0=col0: e.tensor_scalar(
                        out=wint[:, :], in0=wtmp[:, 0:D],
                        scalar1=sbuf["sgx"][:, col0 + oc:col0 + oc + 1],
                        scalar2=None, op0=ALU.mult))
                    prog.step("vector", lambda e: e.tensor_scalar(
                        out=wbf[:, :], in0=wint[:, :], scalar1=-128.0, scalar2=127.0,
                        op0=ALU.max, op1=ALU.min))
                    prog.wait("tensor", S["sv"])
                    prog.mms(S["sp"], [
                        (lambda e, c=c: e.transpose(
                            ptr[:, c * 128:(c + 1) * 128], wbf[:, c * 128:(c + 1) * 128],
                            sbuf["ident"][:, :])) for c in range(NC)])
                    prog.wait("vector", S["sp"])
                    prog.wait("vector", S["sd"])
                    prog.inc("vector", S["sv"], lambda e: e.tensor_copy(
                        out=wtr[:, :], in_=ptr[:, :]))
                    prog.wait("sync", S["sv"])
                    prog.dma("sync", S["sd"], bass.AP(
                        dram[scr], oc * 128, [[I1, 128], [I1 * 128, NC], [1, 128]]),
                        bass.AP(wtr, 0, [[1024, 128], [128, NC], [1, 128]]))
                    prog.wait("sync", S["sd"])
            for dc in range(NC):
                prog.wait("sync", S["sp"], S["sp"].n)
                prog.dma("sync", S["sd"], wtmp[:, :],
                         dram["wdn"][dc * 128:(dc + 1) * 128, :])
                prog.wait("vector", S["sd"])
                for s0, sl in ((0, 1024), (1024, 1024), (2048, 768)):
                    prog.step("vector", lambda e, dc=dc, s0=s0, sl=sl: e.tensor_scalar(
                        out=wint[:, 0:sl], in0=wtmp[:, s0:s0 + sl],
                        scalar1=sbuf["sc_misc"][:, dc:dc + 1], scalar2=None,
                        op0=ALU.mult))
                    prog.step("vector", lambda e, sl=sl: e.tensor_scalar(
                        out=wbf[:, 0:sl], in0=wint[:, 0:sl], scalar1=-128.0,
                        scalar2=127.0, op0=ALU.max, op1=ALU.min))
                    prog.step("vector", lambda e, dc=dc, sl=sl: e.tensor_scalar(
                        out=wbf[:, 0:sl], in0=wbf[:, 0:sl],
                        scalar1=sbuf["sdrow"][:, dc:dc + 1], scalar2=None,
                        op0=ALU.mult))
                    nctr = sl // 128
                    prog.wait("tensor", S["sv"])
                    prog.mms(S["sp"], [
                        (lambda e, c=c: e.transpose(
                            ptr[:, c * 128:(c + 1) * 128], wbf[:, c * 128:(c + 1) * 128],
                            sbuf["ident"][:, :])) for c in range(nctr)])
                    prog.wait("vector", S["sp"])
                    prog.wait("vector", S["sd"])
                    prog.inc("vector", S["sv"], lambda e, nctr=nctr: e.tensor_copy(
                        out=wtr[:, 0:nctr * 128], in_=ptr[:, 0:nctr * 128]))
                    prog.wait("sync", S["sv"])
                    prog.dma("sync", S["sd"], bass.AP(
                        dram["scr_d"], (s0 + 0) * D + dc * 128,
                        [[D, 128], [D * 128, nctr], [1, 128]]),
                        bass.AP(wtr, 0, [[1024, 128], [128, nctr], [1, 128]]))
                    prog.wait("sync", S["sd"])
            prog.dma("sync", S["sd"], sbuf["scal"][0:1, 2:4], dram["clips"][:, :])
            prog.wait("scalar", S["sd"])
            prog.inc("scalar", S["ss"], lambda e: e.activation(
                out=sbuf["scal"][0:1, 4:6], in_=sbuf["scal"][0:1, 2:4], func=AF.Sigmoid))
            prog.flush()

    def emit_ffn1(first):
        _ffn1_commons()
        if first:
            emit_ffn1_prep()
        emit_ln(fp32_scratch=True)
        xq = sbuf["xnT"]
        with ExitStack() as lctx:
            lsb, lps = mk_local(lctx)
            xf = lsb("xf", [128, T], F32)
            xi = lsb("xi", [128, 1024], I32)
            gbuf = lsb("gbuf", [128, NC * 768], BF16)
            ubuf = lsb("ubuf", [128, NC * 768], BF16)
            dbuf = lsb("dbuf", [128, 6 * 1024], BF16)
            gs = [lsb(f"gs{k}", [128, 512], BF16) for k in range(2)]
            us = [lsb(f"us{k}", [128, 512], BF16) for k in range(2)]
            hs = [lsb(f"hs{k}", [128, 512], BF16) for k in range(2)]
            hq = [lsb(f"hq{k}", [128, 6 * 512], BF16) for k in range(2)]
            hmax = lsb("hmax", [128, 2], F32)
            sxb = lsb("sxb", [128, 1], F32)
            shb = lsb("shb", [128, 1], F32)
            sh2 = lsb("sh2", [128, 1], F32)
            pg = [lps(f"pg{k}", [128, 512], F32) for k in range(2)]
            pu = [lps(f"pu{k}", [128, 512], F32) for k in range(2)]
            pd = [lps(f"pd{k}", [128, D], F32) for k in range(2)]
            emit_allreduce_max(sbuf["st_t"][:, 0:1], sbuf["scal"][0:1, 6:7])
            prog.step("vector", lambda e: e.tensor_mul(
                sbuf["scal"][0:1, 7:8], sbuf["scal"][0:1, 6:7], sbuf["scal"][0:1, 4:5]))
            prog.step("vector", lambda e: e.tensor_scalar(
                out=sbuf["scal"][0:1, 7:8], in0=sbuf["scal"][0:1, 7:8],
                scalar1=1.0 / 127.0, scalar2=None, op0=ALU.mult))
            prog.step("vector", lambda e: e.reciprocal(
                sbuf["scal"][0:1, 8:9], sbuf["scal"][0:1, 7:8]))
            bcast_pp(sxb[:, :], sbuf["scal"][0:1, 8:9], pg[0])
            sxv = lsb("sxv", [128, 1], F32)
            bcast_pp(sxv[:, :], sbuf["scal"][0:1, 7:8], pg[0])
            prog.step("vector", lambda e: e.tensor_scalar(
                out=sbuf["sgx"][:, :], in0=sbuf["sgu_sc"][:, :],
                scalar1=sxv[:, :], scalar2=None, op0=ALU.mult))
            prog.flush()
            prog.wait("sync", S["sxf"])
            xq_dma = 0
            for ic in range(NC):
                if xq_dma:
                    prog.wait("sync", S["sv"])
                prog.dma("sync", S["sxf"], xf[:, :],
                         dram["scr_xnT"][ic * 128:(ic + 1) * 128, :])
                xq_dma = S["sxf"].n
                prog.wait("vector", S["sxf"], xq_dma)
                for h2 in range(2):
                    prog.step("vector", lambda e, h2=h2: e.tensor_scalar(
                        out=xi[:, :], in0=xf[:, h2 * 1024:(h2 + 1) * 1024],
                        scalar1=sxb[:, :], scalar2=None, op0=ALU.mult))
                    prog.step("vector", lambda e, ic=ic, h2=h2: e.tensor_scalar(
                        out=xq[:, ic * T + h2 * 1024:ic * T + (h2 + 1) * 1024],
                        in0=xi[:, :], scalar1=-128.0, scalar2=127.0,
                        op0=ALU.max, op1=ALU.min))
            prog.run("vector", lambda e: e.memset(hmax[:, :], 0.0))
            prog.flush()
            ldw = 0
            pg_free = [0, 0]
            pu_free = [0, 0]
            gs_user = [0, 0]
            for pi, (oc0, nch) in enumerate(PIECES):
                prog.wait("sync", S["sp"], S["sp"].n)
                for ic in range(NC):
                    prog.dma("sync", S["sw0"], gbuf[:, ic * 768:ic * 768 + nch * 128],
                             dram["scr_g"][ic * 128:(ic + 1) * 128,
                                           oc0 * 128:(oc0 + nch) * 128])
                    prog.dma("sync", S["sw0"], ubuf[:, ic * 768:ic * 768 + nch * 128],
                             dram["scr_u"][ic * 128:(ic + 1) * 128,
                                           oc0 * 128:(oc0 + nch) * 128])
                ldw = S["sw0"].n
                prog.wait("tensor", S["sw0"], ldw)
                for ci in range(nch):
                    oc = oc0 + ci
                    for tt in range(4):
                        bk = tt % 2
                        if pg_free[bk]:
                            prog.wait("tensor", S["ss"], pg_free[bk])
                            prog.wait("tensor", S["sv"], pu_free[bk])
                        for which, (buf, ps_, ev) in enumerate(
                                ((gbuf, pg, "g"), (ubuf, pu, "u"))):
                            fns = [(lambda e, buf=buf, ps_=ps_, bk=bk, ci=ci, ic=ic,
                                    tt=tt: e.matmul(
                                        ps_[bk][:, :],
                                        buf[:, ic * 768 + ci * 128:ic * 768 + (ci + 1) * 128],
                                        xq[:, ic * T + tt * 512:ic * T + (tt + 1) * 512],
                                        start=(ic == 0), stop=(ic == NC - 1)))
                                   for ic in range(NC)]
                            prog.mms(S["sp"], fns)
                        mmv = S["sp"].n
                        prog.wait("scalar", S["sp"], mmv)
                        prog.wait("scalar", S["ss"])
                        if gs_user[bk]:
                            prog.wait("scalar", S["sv"], gs_user[bk])
                        prog.inc("scalar", S["ss"], lambda e, bk=bk, oc=oc: e.activation(
                            out=gs[bk][:, :], in_=pg[bk][:, :], func=AF.Silu,
                            scale=sbuf["sgx"][:, oc:oc + 1]))
                        sev = S["ss"].n
                        pg_free[bk] = sev
                        prog.wait("vector", S["sp"], mmv)
                        prog.inc("vector", S["sv"], lambda e, bk=bk, oc=oc: e.tensor_scalar(
                            out=us[bk][:, :], in0=pu[bk][:, :],
                            scalar1=sbuf["sgx"][:, 22 + oc:22 + oc + 1], scalar2=None,
                            op0=ALU.mult))
                        pu_free[bk] = S["sv"].n
                        prog.wait("vector", S["ss"], sev)
                        prog.wait("vector", S["sv"])
                        prog.wait("vector", S["sh%d" % bk])
                        prog.inc("vector", S["sv"], lambda e, bk=bk: e.tensor_mul(
                            hs[bk][:, :], gs[bk][:, :], us[bk][:, :]))
                        gs_user[bk] = S["sv"].n
                        prog.wait("vector", S["sv"])
                        prog.step("vector", lambda e, bk=bk: e.reduce_max(
                            hmax[:, 1:2], hs[bk][:, :], axis=AX.X,
                            apply_absolute_value=True))
                        prog.step("vector", lambda e: e.tensor_max(
                            hmax[:, 0:1], hmax[:, 0:1], hmax[:, 1:2]))
                        prog.wait("sync", S["sv"])
                        prog.dma("sync", S["sh%d" % bk], dram["scr_h"][
                            oc * 128:(oc + 1) * 128, tt * 512:(tt + 1) * 512],
                            hs[bk][:, :])
            prog.flush()
            emit_allreduce_max(hmax[:, 0:1], sbuf["scal"][0:1, 9:10])
            prog.step("vector", lambda e: e.tensor_mul(
                sbuf["scal"][0:1, 10:11], sbuf["scal"][0:1, 9:10], sbuf["scal"][0:1, 5:6]))
            prog.step("vector", lambda e: e.tensor_scalar(
                out=sbuf["scal"][0:1, 10:11], in0=sbuf["scal"][0:1, 10:11],
                scalar1=1.0 / 127.0, scalar2=None, op0=ALU.mult))
            prog.step("vector", lambda e: e.reciprocal(
                sbuf["scal"][0:1, 11:12], sbuf["scal"][0:1, 10:11]))
            prog.step("vector", lambda e: e.tensor_scalar(
                out=sbuf["scal"][0:1, 12:13], in0=sbuf["scal"][0:1, 10:11],
                scalar1=0.5, scalar2=None, op0=ALU.mult))
            bcast_pp(shb[:, :], sbuf["scal"][0:1, 11:12], pg[0])
            bcast_pp(sh2[:, :], sbuf["scal"][0:1, 12:13], pg[0])
            prog.flush()
            prog.wait("sync", S["sh0"])
            prog.wait("sync", S["sh1"])
            for pi, (oc0, nch) in enumerate(PIECES):
                prog.wait("sync", S["sp"], S["sp"].n)
                for ci in range(nch):
                    prog.dma("sync", S["sw1"], dbuf[:, ci * 1024:(ci + 1) * 1024],
                             dram["scr_d"][(oc0 + ci) * 128:(oc0 + ci + 1) * 128, :])
                ldw = S["sw1"].n
                prog.wait("tensor", S["sw1"], ldw)
                hq_dma = [0, 0]
                hq_used = [0, 0]
                pd_free = [0, 0]
                for tg in range(4):
                    hb = tg % 2
                    if hq_used[hb]:
                        prog.wait("sync", S["sp"], hq_used[hb])
                    for ci in range(nch):
                        prog.dma("sync", S["sh%d" % hb], hq[hb][:, ci * 512:(ci + 1) * 512],
                                 dram["scr_h"][(oc0 + ci) * 128:(oc0 + ci + 1) * 128,
                                               tg * 512:(tg + 1) * 512])
                    hqd = S["sh%d" % hb].n
                    prog.wait("vector", S["sh%d" % hb], hqd)
                    for ci in range(nch):
                        prog.step("vector", lambda e, hb=hb, ci=ci: e.tensor_scalar(
                            out=xi[:, 0:512], in0=hq[hb][:, ci * 512:(ci + 1) * 512],
                            scalar1=shb[:, :], scalar2=None, op0=ALU.mult))
                        prog.step("vector", lambda e, hb=hb, ci=ci: e.tensor_scalar(
                            out=hq[hb][:, ci * 512:(ci + 1) * 512], in0=xi[:, 0:512],
                            scalar1=-128.0, scalar2=127.0, op0=ALU.max, op1=ALU.min))
                    hq_ready = S["sv"].n
                    prog.wait("tensor", S["sv"], hq_ready)
                    for ts in range(4):
                        j = tg * 4 + ts
                        pbk = ts % 2
                        if pd_free[pbk]:
                            prog.wait("tensor", S["sv"], pd_free[pbk])
                        fns = []
                        for ci in range(nch):
                            for dc2 in range(2):
                                fns.append(lambda e, hb=hb, ci=ci, ts=ts, dc2=dc2,
                                           pbk=pbk, nch=nch: e.matmul(
                                               pd[pbk][:, dc2 * 512:(dc2 + 1) * 512],
                                               hq[hb][:, ci * 512 + ts * 128:
                                                      ci * 512 + (ts + 1) * 128],
                                               dbuf[:, ci * 1024 + dc2 * 512:
                                                    ci * 1024 + (dc2 + 1) * 512],
                                               start=(ci == 0), stop=(ci == nch - 1)))
                        prog.mms(S["sp"], fns)
                        prog.wait("vector", S["sp"], S["sp"].n)
                        prog.inc("vector", S["sv"], lambda e, j=j, pbk=pbk:
                                 e.scalar_tensor_tensor(
                                     out=sbuf["x"][:, j * D:(j + 1) * D],
                                     in0=pd[pbk][:, :], scalar=sh2[:, :],
                                     in1=sbuf["x"][:, j * D:(j + 1) * D],
                                     op0=ALU.mult, op1=ALU.add))
                        pd_free[pbk] = S["sv"].n
                    hq_used[hb] = S["sp"].n
            prog.flush()

    emit_ffn0(True)
    if n_sub >= 2:
        emit_attn(0)
    if n_sub >= 3:
        emit_ffn0(False)
    if n_sub >= 4:
        emit_ffn1(True)
    if n_sub >= 5:
        emit_attn(1)
    if n_sub >= 6:
        emit_ffn1(False)

    for j in range(NT):
        prog.dma("sync", S["so"], dram["y"][j * 128:(j + 1) * 128, :],
                 sbuf["x"][:, j * D:(j + 1) * D])
    prog.wait("sync", S["so"])
    prog.flush()

    ctx.close()
    return nc



def _host_consts():
    c = {}
    c["identity"] = np.eye(128, dtype=BF)
    c["identityf"] = np.eye(128, dtype=np.float32)
    j = np.arange(32).astype(np.float64)
    inv = 1.0 / 10000.0 ** (2.0 * j / 64.0)
    pos = np.arange(T, dtype=np.float64)
    ang = pos[None, :] * inv[:, None]
    cos = np.cos(ang); sin = np.sin(ang)
    Chead = np.concatenate([cos, cos], axis=0)
    Shead = np.concatenate([-sin, sin], axis=0)
    Cfull = np.tile(Chead, (2, 1))
    Sfull = np.tile(Shead, (2, 1))
    c["ropeCq"] = (Cfull / 8.0).astype(BF)
    c["ropeSq"] = (Sfull / 8.0).astype(BF)
    c["mask0"] = attn_mask_bank(0)[0]
    c["mask1"] = attn_mask_bank(1)[0]
    cf = np.zeros((128, 8), dtype=np.float32)
    cf[:, 0] = 0.5; cf[:, 1] = 1.0; cf[:, 2] = -1.0; cf[:, 3] = 1.0 / D
    c["constf"] = cf
    c["ones1"] = np.ones((1, 128), dtype=BF)
    c["ones1f"] = np.ones((1, 128), dtype=np.float32)
    c["onesrow"] = np.ones((1, 512), dtype=BF)
    o2 = np.zeros((128, 2), dtype=BF)
    o2[:64, 0] = 1; o2[64:, 1] = 1
    c["ones2"] = o2
    return c


def _host_inputs(inp):
    tr = lambda a: np.ascontiguousarray(np.asarray(a, dtype=np.float32).T)
    cp = lambda a: np.ascontiguousarray(np.asarray(a, dtype=np.float32))
    g = {}
    g["w1T"] = tr(inp["L0_ffn_w1"]); g["w1n"] = cp(inp["L0_ffn_w1"])
    g["w2T"] = tr(inp["L0_ffn_w2"]); g["w2n"] = cp(inp["L0_ffn_w2"])
    g["b1pp"] = cp(np.asarray(inp["L0_ffn_b1"]).reshape(32, 128).T)
    g["b2row"] = cp(np.asarray(inp["L0_ffn_b2"]).reshape(1, D))
    g["wqT"] = cp(inp["L0_attn_wq"]); g["wkT"] = cp(inp["L0_attn_wk"])
    g["wvT"] = cp(inp["L0_attn_wv"]); g["woT"] = cp(inp["L0_attn_wo"])
    g["bqrow"] = cp(np.asarray(inp["L0_attn_bq"]).reshape(1, D))
    g["bkrow"] = cp(np.asarray(inp["L0_attn_bk"]).reshape(1, D))
    g["bvrow"] = cp(np.asarray(inp["L0_attn_bv"]).reshape(1, D))
    g["borow"] = cp(np.asarray(inp["L0_attn_bo"]).reshape(1, D))
    wa = np.asarray(inp["L1_attn_w_attn"], dtype=np.float32)
    g["aqT"] = tr(wa[0:D]); g["akT"] = tr(wa[D:2 * D]); g["avT"] = tr(wa[2 * D:3 * D])
    g["apT"] = tr(inp["L1_attn_w_proj"])
    g["wgT"] = tr(inp["L1_ffn_w_gate"]); g["wgn"] = cp(inp["L1_ffn_w_gate"])
    g["wuT"] = tr(inp["L1_ffn_w_up"]); g["wun"] = cp(inp["L1_ffn_w_up"])
    g["wdT"] = tr(inp["L1_ffn_w_down"]); g["wdn"] = cp(inp["L1_ffn_w_down"])
    perm64 = np.concatenate([np.arange(0, 64, 2), np.arange(1, 64, 2)])
    permD = (np.arange(D).reshape(16, 64)[:, perm64]).ravel()
    for nm in ("wqT", "wkT"):
        g[nm] = np.ascontiguousarray(g[nm][:, permD])
    g["bqrow"] = np.ascontiguousarray(g["bqrow"][:, permD])
    g["bkrow"] = np.ascontiguousarray(g["bkrow"][:, permD])
    for nm in ("wqT", "wkT", "wvT", "aqT", "akT", "avT"):
        g[nm + "s"] = np.ascontiguousarray(
            g[nm].reshape(D, 8, 128).transpose(1, 0, 2))
    g["clips"] = np.array([[float(np.asarray(inp["L1_ffn_clip_x"]).ravel()[0]),
                            float(np.asarray(inp["L1_ffn_clip_down"]).ravel()[0])]],
                          dtype=np.float32)
    return g


def kernel(**inputs):
    x = np.asarray(inputs["L0_attn_x"], dtype=np.float32)
    shared = _host_inputs(inputs)
    shared.update(_host_consts())
    nc = build_program(N_SUB)
    in_maps = []
    for core in range(8):
        m = dict(shared)
        m["x_in"] = np.ascontiguousarray(x[core])
        in_maps.append(m)
    res = run_bass_kernel_spmd(nc, in_maps, core_ids=list(range(8)))
    out = np.stack([np.asarray(r["y"]) for r in res.results], axis=0)
    return out.astype(np.float32)
